# version 13
# speedup vs baseline: 1.2836x; 1.0100x over previous
import contextlib
import numpy as np
import ml_dtypes
import concourse.bass as bass
import concourse.mybir as mybir
from concourse.bass_utils import run_bass_kernel_spmd

F32 = mybir.dt.float32
F16 = mybir.dt.float16
BF16 = mybir.dt.bfloat16
AF = mybir.ActivationFunctionType
ALU = mybir.AluOpType
AX = mybir.AxisListType

T = 4096
D = 1024
NT = T // 128
I0 = 2816
HEADS = 16
DH = 64
E = 8
HM = 2048
L1DEPTH = 6
EPS_LN = 1e-5


class Sem:
    def __init__(self, es, nc, name):
        self.h = es.enter_context(nc.semaphore(name))
        self.n = 0

    @classmethod
    def wrap(cls, h):
        s = cls.__new__(cls)
        s.h = h
        s.n = 0
        return s


class B:
    def __init__(self, name):
        self.nc = bass.Bass("TRN2", target_bir_lowering=False)
        self.es = contextlib.ExitStack()
        self.cur = self.es
        self.pfx = ""
        self.name = name
        self.pool = None
        self.bar = None
        self.barn = 0

    def sb(self, name, shape, dt):
        return self.cur.enter_context(self.nc.sbuf_tensor(self.pfx + name, shape, dt))

    def ps(self, name, shape, dt=F32):
        return self.cur.enter_context(self.nc.psum_tensor(self.pfx + name, shape, dt))

    def sem(self, name):
        if self.pool is None:
            return Sem(self.es, self.nc, self.pfx + name)
        if name not in self.pool:
            self.pool[name] = self.es.enter_context(self.nc.semaphore("pool_" + name))
        return Sem.wrap(self.pool[name])

    def din(self, name, shape, dt=F32):
        return self.nc.dram_tensor(name, shape, dt, kind="ExternalInput")

    def dout(self, name, shape, dt=F32):
        return self.nc.dram_tensor(name, shape, dt, kind="ExternalOutput")

    def dint(self, name, shape, dt=BF16):
        return self.nc.dram_tensor(self.pfx + name, shape, dt)


def emit_barrier(nc, entry, b=None):
    if entry is None:
        if b is not None and b.pool is not None and b.barn == 0:
            pass
        return
    sem, val = entry
    if b is not None and b.pool is not None:
        b.barn += 1
        with nc.Block() as blk:
            @blk.sync
            def _(sync):
                sync.wait_ge(sem, val)
                for h in b.pool.values():
                    sync.sem_clear(h)
                sync.sem_inc(b.bar, 1)
            for en in ("tensor", "vector", "scalar", "gpsimd"):
                getattr(blk, en)(lambda eng, v=b.barn: eng.wait_ge(b.bar, v))
        return
    with nc.Block() as blk:
        for en in ("sync", "tensor", "vector", "scalar", "gpsimd"):
            getattr(blk, en)(lambda eng, s=sem, v=val: eng.wait_ge(s, v))


def v16(t, W):
    return t[:, 0:W].rearrange("p (b k) -> p b k", k=16)


def b3(t, W):
    return t[:, 0:W // 16].unsqueeze(2).broadcast_to([128, W // 16, 16])


QUANT_SC_OPS = 4
QUANT_DVE_OPS = 11
QUANT_DVE_TOT = QUANT_SC_OPS + QUANT_DVE_OPS


def emit_quant_scales(vector, dv, src3, bm, sc, rs, hs, W):
    NB = W // 16
    vector.tensor_reduce(bm[:, 0:NB], src3, axis=AX.X, op=ALU.max,
                         apply_absolute_value=True).then_inc(dv.h, 1)
    dv.n += 1
    vector.wait_ge(dv.h, dv.n)
    vector.tensor_scalar(sc[:, 0:NB], bm[:, 0:NB], 1.0 / 6.0, 1e-12,
                         ALU.mult, ALU.max).then_inc(dv.h, 1)
    dv.n += 1
    vector.wait_ge(dv.h, dv.n)
    vector.reciprocal(rs[:, 0:NB], sc[:, 0:NB]).then_inc(dv.h, 1)
    dv.n += 1
    vector.tensor_scalar_mul(hs[:, 0:NB], sc[:, 0:NB], 0.5).then_inc(dv.h, 1)
    dv.n += 1
    vector.wait_ge(dv.h, dv.n)


def emit_quant_dve(vector, dv, src, ab, sg, acc, vv, rs, hs, dst, W):
    vector.tensor_tensor(v16(vv, W), v16(ab, W), b3(rs, W), ALU.mult).then_inc(dv.h, 1)
    dv.n += 1
    vector.tensor_scalar(acc[:, 0:W], vv[:, 0:W], 5.0, 2.0, ALU.is_gt, ALU.mult).then_inc(dv.h, 1)
    dv.n += 1
    for c in (3.5, 2.5):
        vector.scalar_tensor_tensor(acc[:, 0:W], vv[:, 0:W], c, acc[:, 0:W],
                                    ALU.is_gt, ALU.add).then_inc(dv.h, 1)
        dv.n += 1
    vector.tensor_tensor(acc[:, 0:W], acc[:, 0:W], acc[:, 0:W], ALU.add).then_inc(dv.h, 1)
    dv.n += 1
    for c in (1.75, 1.25, 0.75, 0.25):
        vector.scalar_tensor_tensor(acc[:, 0:W], vv[:, 0:W], c, acc[:, 0:W],
                                    ALU.is_gt, ALU.add).then_inc(dv.h, 1)
        dv.n += 1
    vector.tensor_tensor(v16(vv, W), v16(sg, W), b3(hs, W), ALU.mult).then_inc(dv.h, 1)
    dv.n += 1
    vector.wait_ge(dv.h, dv.n)
    vector.tensor_tensor(dst[:, 0:W], acc[:, 0:W], vv[:, 0:W], ALU.mult).then_inc(dv.h, 1)
    dv.n += 1


def emit_quant_chain_gps(gpsimd, gq, vv, acc, mg, W):
    gpsimd.tensor_scalar(acc[:, 0:W], vv[:, 0:W], 5.0, 2.0, ALU.is_gt, ALU.mult).then_inc(gq.h, 1)
    gq.n += 1
    for c in (3.5, 2.5):
        gpsimd.tensor_single_scalar(mg[:, 0:W], vv[:, 0:W], c, ALU.is_gt).then_inc(gq.h, 1)
        gq.n += 1
        gpsimd.tensor_tensor(acc[:, 0:W], acc[:, 0:W], mg[:, 0:W], ALU.add).then_inc(gq.h, 1)
        gq.n += 1
    gpsimd.tensor_tensor(acc[:, 0:W], acc[:, 0:W], acc[:, 0:W], ALU.add).then_inc(gq.h, 1)
    gq.n += 1
    for c in (1.75, 1.25, 0.75, 0.25):
        gpsimd.tensor_single_scalar(mg[:, 0:W], vv[:, 0:W], c, ALU.is_gt).then_inc(gq.h, 1)
        gq.n += 1
        gpsimd.tensor_tensor(acc[:, 0:W], acc[:, 0:W], mg[:, 0:W], ALU.add).then_inc(gq.h, 1)
        gq.n += 1


GPS_CHAIN_OPS = 14


def emit_quant_pre_dve(vector, dv, ab, vv, rs, W):
    vector.tensor_tensor(v16(vv, W), v16(ab, W), b3(rs, W), ALU.mult).then_inc(dv.h, 1)
    dv.n += 1


def emit_quant_post_dve(vector, dv, sg, tbuf, acc, hs, dst, W):
    vector.tensor_tensor(v16(tbuf, W), v16(sg, W), b3(hs, W), ALU.mult).then_inc(dv.h, 1)
    dv.n += 1
    vector.wait_ge(dv.h, dv.n)
    vector.tensor_tensor(dst[:, 0:W], acc[:, 0:W], tbuf[:, 0:W], ALU.mult).then_inc(dv.h, 1)
    dv.n += 1


def emit_hilo(vector, dv, src_f, dst_hi, dst_lo, W):
    vector.tensor_copy(dst_hi[:, 0:W], src_f[:, 0:W]).then_inc(dv.h, 1)
    dv.n += 1
    vector.wait_ge(dv.h, dv.n)
    vector.tensor_tensor(dst_lo[:, 0:W], src_f[:, 0:W], dst_hi[:, 0:W], ALU.subtract).then_inc(dv.h, 1)
    dv.n += 1



def build_prep():
    b = B("prep")
    nc = b.nc
    wg = b.din("wg", [I0, D])
    wu = b.din("wu", [I0, D])
    wd = b.din("wd", [D, I0])
    wgq = b.dout("wgq", [I0, D], BF16)
    wgql = b.dout("wgql", [I0, D], BF16)
    wuq = b.dout("wuq", [I0, D], BF16)
    wuql = b.dout("wuql", [I0, D], BF16)
    wdq = b.dout("wdq", [D, I0], BF16)
    emit_prep(b, wg, wu, wd, wgq, wgql, wuq, wuql, wdq, None)
    b.es.close()
    return nc


class BGCast:

    def __init__(self, b, name, jobs):
        self.sem = b.sem(name)
        self.jobs = jobs
        self.pos = 0
        self.sem.n = 16 * len(self.jobs)

    def emit(self, gpsimd, k):
        for _ in range(k):
            if self.pos >= len(self.jobs):
                return
            dst, src = self.jobs[self.pos]
            gpsimd.dma_start(out=dst, in_=src).then_inc(self.sem.h, 16)
            self.pos += 1

    def finish(self, gpsimd):
        self.emit(gpsimd, len(self.jobs) - self.pos)


def moe_cast_jobs(w1s, w2s, w3s, w1_b, w2_b, w3_b):
    jobs = []
    for e in range(E):
        for k in range(8):
            jobs.append((w1_b[e, k * 128:(k + 1) * 128, :], w1s[e, k * 128:(k + 1) * 128, :]))
        for k in range(8):
            jobs.append((w2_b[e, k * 128:(k + 1) * 128, :], w2s[e, k * 128:(k + 1) * 128, :]))
        for k in range(16):
            jobs.append((w3_b[e, k * 128:(k + 1) * 128, :], w3s[e, k * 128:(k + 1) * 128, :]))
    return jobs


def l1_cast_jobs(w_qkv, w_out, wqkv_b, wout_b):
    jobs = []
    for l in range(L1DEPTH):
        for k in range(8):
            jobs.append((wqkv_b[l, k * 128:(k + 1) * 128, :], w_qkv[l, k * 128:(k + 1) * 128, :]))
    for l in range(L1DEPTH):
        for k in range(8):
            jobs.append((wout_b[l, k * 128:(k + 1) * 128, :], w_out[l, k * 128:(k + 1) * 128, :]))
    return jobs


def emit_prep(b, wg, wu, wd, wgq, wgql, wuq, wuql, wdq, entry, bg=None):
    nc = b.nc
    emit_barrier(nc, entry, b)
    jobs = []
    for i in range(I0 // 128):
        jobs.append((wg, wgq, i * 128, D))
        jobs.append((wu, wuq, i * 128, D))
    for i in range(D // 128):
        jobs.append((wd, wdq, i * 128, I0))
    NJ = len(jobs)

    W_MAX = I0
    xt = [b.sb(f"xt{p}", [128, W_MAX], F32) for p in range(2)]
    ab = b.sb("ab", [128, W_MAX], F32)
    sg = b.sb("sg", [128, W_MAX], F32)
    acc = b.sb("acc", [128, W_MAX], F16)
    vv = b.sb("vv", [128, W_MAX], F16)
    bm = b.sb("bm", [128, W_MAX // 16], F32)
    sc = b.sb("sc", [128, W_MAX // 16], F32)
    rs = b.sb("rs", [128, W_MAX // 16], F32)
    hs = b.sb("hs", [128, W_MAX // 16], F32)
    qt = [b.sb(f"qt{p}", [128, W_MAX], BF16) for p in range(2)]

    ld = b.sem("ld")
    dv = b.sem("dv")
    ac = b.sem("ac")
    st = b.sem("st")
    DVJ = QUANT_DVE_TOT

    with nc.Block() as block:
        @block.sync
        def _(sync):
            for i, (src, dst, r0, W) in enumerate(jobs):
                if i >= 2:
                    sync.wait_ge(dv.h, (i - 1) * DVJ)
                sync.dma_start(out=xt[i % 2][:, 0:W], in_=src[r0:r0 + 128, :]).then_inc(ld.h, 16)
        ld.n = 16 * NJ

        @block.scalar
        def _(scalar):
            for i, (src, dst, r0, W) in enumerate(jobs):
                scalar.wait_ge(ld.h, 16 * (i + 1))
                if i >= 1:
                    scalar.wait_ge(dv.h, i * DVJ)
                scalar.activation(ab[:, 0:W], xt[i % 2][:, 0:W], AF.Abs).then_inc(ac.h, 1)
                scalar.sign(sg[:, 0:W], xt[i % 2][:, 0:W]).then_inc(ac.h, 1)
        ac.n = 2 * NJ

        @block.vector
        def _(vector):
            for i, (src, dst, r0, W) in enumerate(jobs):
                vector.wait_ge(ld.h, 16 * (i + 1))
                vector.wait_ge(ac.h, 2 * (i + 1))
                if i >= 2:
                    vector.wait_ge(st.h, 16 * (i - 1))
                emit_quant_scales(vector, dv, v16(xt[i % 2], W), bm, sc, rs, hs, W)
                emit_quant_dve(vector, dv, xt[i % 2], ab, sg, acc, vv, rs, hs, qt[i % 2], W)
        assert dv.n == DVJ * NJ

        @block.gpsimd
        def _(gpsimd):
            for i, (src, dst, r0, W) in enumerate(jobs):
                gpsimd.wait_ge(dv.h, DVJ * (i + 1))
                gpsimd.dma_start(out=dst[r0:r0 + 128, :], in_=qt[i % 2][:, 0:W]).then_inc(st.h, 16)
                if bg is not None:
                    bg.emit(gpsimd, 1 if i < 12 else 3)
            gpsimd.wait_ge(st.h, 16 * NJ)
        st.n = 16 * NJ

    return (st.h, st.n)



def build_l0ffn():
    b = B("l0ffn")
    nc = b.nc
    x_in = b.din("x_in", [T, D])
    wgq = b.din("wgq", [I0, D], BF16)
    wgql = b.din("wgql", [I0, D], BF16)
    wuq = b.din("wuq", [I0, D], BF16)
    wuql = b.din("wuql", [I0, D], BF16)
    wdq = b.din("wdq", [D, I0], BF16)
    x_out = b.dout("x_out", [T, D])
    emit_l0ffn(b, x_in, x_out, wgq, wgql, wuq, wuql, wdq, None)
    b.es.close()
    return nc


def emit_l0ffn(b, x_in, x_out, wgq, wgql, wuq, wuql, wdq, entry, bg=None):
    nc = b.nc
    emit_barrier(nc, entry, b)
    hq_d = b.dint("hq_d", [T, D])
    hql_d = b.dint("hql_d", [T, D])
    hidq_d = b.dint("hidq_d", [T, I0])

    es12 = contextlib.ExitStack()
    _sb_outer = b.sb
    def _sb12(name, shape, dt):
        return es12.enter_context(b.nc.sbuf_tensor(b.pfx + name, shape, dt))
    def _ps12(name, shape, dt=F32):
        return es12.enter_context(b.nc.psum_tensor(b.pfx + name, shape, dt))
    b_sb, b_ps = b.sb, b.ps
    b.sb, b.ps = (lambda n, s, d: _sb12(n, s, d)), (lambda n, s, d=F32: _ps12(n, s, d))
    xt = [b.sb(f"xt{p}", [128, D], F32) for p in range(2)]
    ht0 = b.sb("ht0", [128, D], F32)
    ht = [ht0, ht0]
    hqt = [b.sb(f"hqt{p}", [128, D], BF16) for p in range(2)]
    hqlt = [b.sb(f"hqlt{p}", [128, D], BF16) for p in range(2)]
    hqf = b.sb("hqf", [128, D], F32)
    sqs = hqf
    stat = b.sb("stat", [128, 8], F32)
    ab1 = b.sb("ab1", [128, D], F32)
    sg1 = b.sb("sg1", [128, D], F32)

    ld = b.sem("ld")
    dv = b.sem("dv")
    ac = b.sem("ac")
    st = b.sem("st")
    DV1 = 7 + QUANT_DVE_TOT
    AC1 = 4

    NCH = [(j * 512, min(512, I0 - j * 512)) for j in range((I0 + 511) // 512)]
    NCHN = len(NCH)
    hqT = [[[b.sb(f"hqT{p}_{hl}_{k}", [128, 512], BF16) for k in range(8)] for hl in range(2)] for p in range(2)]
    wgt = [[[b.sb(f"wgt{p}_{hl}_{k}", [128, 512], BF16) for k in range(8)] for hl in range(2)] for p in range(2)]
    wut = [[[b.sb(f"wut{p}_{hl}_{k}", [128, 512], BF16) for k in range(8)] for hl in range(2)] for p in range(2)]
    pg = [b.ps(f"pg{t}", [128, 512]) for t in range(4)]
    pu = [b.ps(f"pu{t}", [128, 512]) for t in range(4)]
    sgl = [b.sb(f"sgl{t}", [128, 512], F32) for t in range(2)]
    hidf = [b.sb(f"hidf{t}", [128, I0], F32) for t in range(4)]
    hq2 = b.sb("hq2", [128, I0], BF16)
    HW = I0 // 2
    ab2 = b.sb("ab2", [128, HW], F32)
    sg2 = b.sb("sg2", [128, HW], F32)
    acc2 = b.sb("acc2", [128, HW], F16)
    vv2 = b.sb("vv2", [128, HW], F16)
    bm2 = b.sb("bm2", [128, HW // 16], F32)
    sc2 = b.sb("sc2", [128, HW // 16], F32)
    rs2 = b.sb("rs2", [128, HW // 16], F32)
    hs2 = b.sb("hs2", [128, HW // 16], F32)

    ldw = b.sem("ldw")
    ld2 = ldw
    pe2 = b.sem("pe2")
    dv2 = b.sem("dv2")
    ac2 = b.sem("ac2")
    st2 = b.sem("st2")

    DV2_NC = 4
    DV2_QT = 2 * (QUANT_SC_OPS + QUANT_DVE_OPS)
    dv2_mult = {}
    dv2_qdone = {}
    c = 0
    for tg in range(8):
        for j in range(NCHN):
            for t in range(4):
                c += 1
                dv2_mult[(tg, j, t)] = c
        for t in range(4):
            c += DV2_QT
            dv2_qdone[(tg, t)] = c
    DV2_TOT = c
    ac2_silu = {}
    ac2_absg = {}
    c = 0
    for tg in range(8):
        for j in range(NCHN):
            for t in range(4):
                c += 1
                ac2_silu[(tg, j, t)] = c
        for t in range(4):
            c += 4
            ac2_absg[(tg, t)] = c
    AC2_TOT = c
    pe2_done = {}
    c = 0
    for tg in range(8):
        for j in range(NCHN):
            c += 1
            pe2_done[(tg, j)] = c
    PE2_TOT = c

    K2 = I0 // 128
    ld3 = b.sem("ld3")
    ld3w = ld3
    pe3 = b.sem("pe3")
    dv3 = b.sem("dv3")
    st3 = b.sem("st3")

    pe3_done = {}
    dv3_done = {}
    c = 0
    for fc in range(2):
        for tg in range(8):
            for t in range(4):
                c += 1
                pe3_done[(fc, tg, t)] = c
                dv3_done[(fc, tg, t)] = c
    PE3_TOT = c

    with nc.Block() as block:
        @block.sync
        def _(sync):
            for i in range(NT):
                if i >= 2:
                    sync.wait_ge(dv.h, (i - 1) * DV1)
                sync.dma_start(out=xt[i % 2][:], in_=x_in[i * 128:(i + 1) * 128, :]).then_inc(ld.h, 16)
            for tg in range(8):
                sync.wait_ge(st.h, 16 * min(4 * tg + 4, NT))
                if tg >= 2:
                    sync.wait_ge(pe2.h, pe2_done[(tg - 2, NCHN - 1)])
                for k in range(8):
                    sync.dma_start(out=hqT[tg % 2][0][k][:],
                                   in_=hq_d[tg * 512:(tg + 1) * 512, k * 128:(k + 1) * 128],
                                   transpose=True).then_inc(ld2.h, 16)
                for j, (n0, nw) in enumerate(NCH):
                    it = tg * NCHN + j
                    if it >= 2:
                        ptg, pj = divmod(it - 2, NCHN)
                        sync.wait_ge(pe2.h, pe2_done[(ptg, pj)])
                    for k in range(8):
                        sync.dma_start(out=wgt[it % 2][0][k][:, 0:nw],
                                       in_=wgq[n0:n0 + nw, k * 128:(k + 1) * 128],
                                       transpose=True).then_inc(ldw.h, 16)
                        sync.dma_start(out=wut[it % 2][0][k][:, 0:nw],
                                       in_=wuq[n0:n0 + nw, k * 128:(k + 1) * 128],
                                       transpose=True).then_inc(ldw.h, 16)
            ld.n = 16 * NT
            ldw.n = 16 * 8 * (8 + NCHN * 16)
            ld3w.n = 16 * 2 * K2
            ld3.n = 16 * 2 * 8 * (K2 + 4)

        @block.scalar
        def _(scalar):
            for i in range(NT):
                base = i * DV1
                abase = i * AC1
                scalar.wait_ge(ld.h, 16 * (i + 1))
                if i >= 1:
                    scalar.wait_ge(dv.h, i * DV1)
                scalar.activation(sqs[:], xt[i % 2][:], AF.Square,
                                  accum_out=stat[:, 1:2]).then_inc(ac.h, 1)
                scalar.wait_ge(dv.h, base + 5)
                scalar.activation(stat[:, 6:7], stat[:, 5:6], AF.Sqrt).then_inc(ac.h, 1)
                scalar.wait_ge(dv.h, base + 7)
                scalar.activation(ab1[:], ht[i % 2][:], AF.Abs).then_inc(ac.h, 1)
                scalar.sign(sg1[:], ht[i % 2][:]).then_inc(ac.h, 1)
            ac.n = AC1 * NT
            for tg in range(8):
                for j, (n0, nw) in enumerate(NCH):
                    scalar.wait_ge(pe2.h, pe2_done[(tg, j)])
                    for t in range(4):
                        g = (tg * NCHN + j) * 4 + t
                        if g >= 2:
                            pg_, r = divmod(g - 2, 4)
                            ptg, pj = divmod(pg_, NCHN)
                            scalar.wait_ge(dv2.h, dv2_mult[(ptg, pj, r)])
                        scalar.activation(sgl[g % 2][:, 0:nw], pg[t][:, 0:nw],
                                          AF.Silu).then_inc(ac2.h, 1)
                for t in range(4):
                    scalar.wait_ge(dv2.h, dv2_mult[(tg, NCHN - 1, t)])
                    for hf in range(2):
                        if hf == 1:
                            scalar.wait_ge(dv2.h, dv2_qdone[(tg, t)] - QUANT_DVE_TOT)
                        elif t >= 1:
                            scalar.wait_ge(dv2.h, dv2_qdone[(tg, t - 1)])
                        elif tg >= 1:
                            scalar.wait_ge(dv2.h, dv2_qdone[(tg - 1, 3)])
                        scalar.activation(ab2[:, 0:HW], hidf[t][:, hf * HW:(hf + 1) * HW],
                                          AF.Abs).then_inc(ac2.h, 1)
                        scalar.sign(sg2[:, 0:HW], hidf[t][:, hf * HW:(hf + 1) * HW]).then_inc(ac2.h, 1)
            ac2.n = AC2_TOT

        @block.tensor
        def _(tensor):
            for tg in range(8):
                tensor.wait_ge(ld2.h, 16 * (104 * tg + 8))
                for j, (n0, nw) in enumerate(NCH):
                    it = tg * NCHN + j
                    tensor.wait_ge(ldw.h, 16 * (104 * tg + 8 + 16 * (j + 1)))
                    if it >= 1:
                        ptg, pj = divmod(it - 1, NCHN)
                        tensor.wait_ge(ac2.h, ac2_silu[(ptg, pj, 3)])
                        tensor.wait_ge(dv2.h, dv2_mult[(ptg, pj, 3)])
                    mm = None
                    for k in range(8):
                        for t in range(4):
                            lh = hqT[tg % 2][0][k][:, t * 128:(t + 1) * 128]
                            tensor.matmul(pg[t][:, 0:nw], lh, wgt[it % 2][0][k][:, 0:nw],
                                          start=(k == 0), stop=(k == 7))
                            mm = tensor.matmul(pu[t][:, 0:nw], lh, wut[it % 2][0][k][:, 0:nw],
                                               start=(k == 0), stop=(k == 7))
                    mm.then_inc(pe2.h, 1)
            pe2.n = PE2_TOT

        @block.vector
        def _(vector):
            for i in range(NT):
                abase = i * AC1
                vector.wait_ge(ld.h, 16 * (i + 1))
                vector.tensor_reduce(stat[:, 0:1], xt[i % 2][:], axis=AX.X, op=ALU.add).then_inc(dv.h, 1)
                dv.n += 1
                vector.wait_ge(dv.h, dv.n)
                vector.tensor_scalar_mul(stat[:, 2:3], stat[:, 0:1], -1.0 / D).then_inc(dv.h, 1)
                dv.n += 1
                vector.wait_ge(dv.h, dv.n)
                vector.tensor_tensor(stat[:, 3:4], stat[:, 2:3], stat[:, 2:3], ALU.mult).then_inc(dv.h, 1)
                dv.n += 1
                vector.wait_ge(ac.h, abase + 1)
                vector.wait_ge(dv.h, dv.n)
                vector.tensor_scalar(stat[:, 4:5], stat[:, 1:2], 1.0 / D, EPS_LN, ALU.mult, ALU.add).then_inc(dv.h, 1)
                dv.n += 1
                vector.wait_ge(dv.h, dv.n)
                vector.tensor_tensor(stat[:, 5:6], stat[:, 4:5], stat[:, 3:4], ALU.subtract).then_inc(dv.h, 1)
                dv.n += 1
                vector.wait_ge(ac.h, abase + 2)
                vector.reciprocal(stat[:, 7:8], stat[:, 6:7]).then_inc(dv.h, 1)
                dv.n += 1
                vector.wait_ge(dv.h, dv.n)
                vector.tensor_scalar(ht[i % 2][:], xt[i % 2][:], stat[:, 2:3], stat[:, 7:8],
                                     ALU.add, ALU.mult).then_inc(dv.h, 1)
                dv.n += 1
                emit_quant_scales(vector, dv, v16(ht[i % 2], D), bm2, sc2, rs2, hs2, D)
                vector.wait_ge(ac.h, abase + 4)
                if i >= 2:
                    vector.wait_ge(st.h, 16 * (i - 1))
                emit_quant_dve(vector, dv, ht[i % 2], ab1, sg1, acc2, vv2, rs2, hs2,
                               hqt[i % 2], D)
            assert dv.n == DV1 * NT, (dv.n, DV1 * NT)
            for tg in range(8):
                for j, (n0, nw) in enumerate(NCH):
                    vector.wait_ge(pe2.h, pe2_done[(tg, j)])
                    for t in range(4):
                        g = (tg * NCHN + j) * 4 + t
                        vector.wait_ge(ac2.h, ac2_silu[(tg, j, t)])
                        vector.tensor_tensor(hidf[t][:, n0:n0 + nw], sgl[g % 2][:, 0:nw],
                                             pu[t][:, 0:nw], ALU.mult).then_inc(dv2.h, 1)
                        dv2.n += 1
                for t in range(4):
                    gq = tg * 4 + t
                    if gq >= 1:
                        vector.wait_ge(st2.h, 16 * gq)
                    for hf in range(2):
                        vector.wait_ge(ac2.h, ac2_absg[(tg, t)] - 2 + 2 * hf)
                        hsl = hidf[t][:, hf * HW:(hf + 1) * HW]
                        emit_quant_scales(vector, dv2, v16(hsl, HW), bm2, sc2, rs2, hs2, HW)
                        emit_quant_dve(vector, dv2, hsl, ab2, sg2, acc2, vv2, rs2, hs2,
                                       hq2[:, hf * HW:(hf + 1) * HW], HW)
            assert dv2.n == DV2_TOT, (dv2.n, DV2_TOT)

        @block.gpsimd
        def _(gpsimd):
            for i in range(NT):
                gpsimd.wait_ge(dv.h, DV1 * (i + 1))
                gpsimd.dma_start(out=hq_d[i * 128:(i + 1) * 128, :], in_=hqt[i % 2][:]).then_inc(st.h, 16)
                if bg is not None:
                    bg.emit(gpsimd, 2)
            st.n = 16 * NT
            for tg in range(8):
                for t in range(4):
                    gpsimd.wait_ge(dv2.h, dv2_qdone[(tg, t)])
                    r0 = (tg * 4 + t) * 128
                    gpsimd.dma_start(out=hidq_d[r0:r0 + 128, :], in_=hq2[:]).then_inc(st2.h, 16)
                    if bg is not None:
                        bg.emit(gpsimd, 2)
            st2.n = 16 * 32

    es12.close()
    b.sb, b.ps = b_sb, b_ps
    wdt = [b.sb(f"wdt{k}", [128, 512], BF16) for k in range(K2)]
    hdt = [[b.sb(f"hdt{p}_{k}", [128, 512], BF16) for k in range(K2)] for p in range(2)]
    xr3 = [[b.sb(f"xr3{p}_{t}", [128, 512], F32) for t in range(4)] for p in range(2)]
    py = [b.ps(f"py{t}", [128, 512]) for t in range(4)]
    xo3 = [b.sb(f"xo3{g}", [128, 512], F32) for g in range(2)]

    with nc.Block() as blockB:
        @blockB.sync
        def _(sync):
            sync.wait_ge(st2.h, 16 * 32)
            for fc in range(2):
                if fc == 1:
                    sync.wait_ge(pe3.h, pe3_done[(0, 7, 3)])
                for k in range(K2):
                    sync.dma_start(out=wdt[k][:],
                                   in_=wdq[fc * 512:(fc + 1) * 512, k * 128:(k + 1) * 128],
                                   transpose=True).then_inc(ld3w.h, 16)
                for tg in range(8):
                    g = fc * 8 + tg
                    if g >= 2:
                        sync.wait_ge(pe3.h, pe3_done[((g - 2) // 8, (g - 2) % 8, 3)])
                    for k in range(K2):
                        sync.dma_start(out=hdt[tg % 2][k][:],
                                       in_=hidq_d[tg * 512:(tg + 1) * 512, k * 128:(k + 1) * 128],
                                       transpose=True).then_inc(ld3.h, 16)
                    for t in range(4):
                        if g >= 2:
                            sync.wait_ge(dv3.h, dv3_done[((g - 2) // 8, (g - 2) % 8, t)])
                        r0 = (tg * 4 + t) * 128
                        sync.dma_start(out=xr3[tg % 2][t][:],
                                       in_=x_in[r0:r0 + 128, fc * 512:(fc + 1) * 512]).then_inc(ld3.h, 16)

        @blockB.tensor
        def _(tensor):
            for fc in range(2):
                tensor.wait_ge(ld3.h, 16 * (fc * (K2 + 8 * (K2 + 4)) + K2))
                for tg in range(8):
                    g = fc * 8 + tg
                    tensor.wait_ge(ld3.h, 16 * ((fc + 1) * K2 + g * (K2 + 4) + K2))
                    for t in range(4):
                        if g >= 1:
                            tensor.wait_ge(dv3.h, dv3_done[((g - 1) // 8, (g - 1) % 8, t)])
                        mm = None
                        for k in range(K2):
                            mm = tensor.matmul(py[t][:], hdt[tg % 2][k][:, t * 128:(t + 1) * 128],
                                               wdt[k][:], start=(k == 0), stop=(k == K2 - 1))
                        mm.then_inc(pe3.h, 1)
            pe3.n = PE3_TOT

        @blockB.vector
        def _(vector):
            for fc in range(2):
                for tg in range(8):
                    g = fc * 8 + tg
                    for t in range(4):
                        vector.wait_ge(pe3.h, pe3_done[(fc, tg, t)])
                        vector.wait_ge(ld3.h, 16 * ((fc + 1) * K2 + g * (K2 + 4) + K2 + t + 1))
                        gg = g * 4 + t
                        if gg >= 2:
                            vector.wait_ge(st3.h, 16 * (gg - 1))
                        vector.scalar_tensor_tensor(xo3[gg % 2][:], py[t][:], 0.5,
                                                    xr3[tg % 2][t][:], ALU.mult,
                                                    ALU.add).then_inc(dv3.h, 1)
                        dv3.n += 1
            assert dv3.n == PE3_TOT

        @blockB.gpsimd
        def _(gpsimd):
            for fc in range(2):
                for tg in range(8):
                    for t in range(4):
                        g = (fc * 8 + tg) * 4 + t
                        gpsimd.wait_ge(dv3.h, dv3_done[(fc, tg, t)])
                        r0 = (tg * 4 + t) * 128
                        gpsimd.dma_start(out=x_out[r0:r0 + 128, fc * 512:(fc + 1) * 512],
                                         in_=xo3[g % 2][:]).then_inc(st3.h, 16)
            if bg is not None:
                bg.finish(gpsimd)
            st3.n = 16 * 2 * 32
            gpsimd.wait_ge(st3.h, st3.n)

    return (st3.h, 16 * 2 * 32)



def build_l0attn():
    b = B("l0attn")
    nc = b.nc
    x_in = b.din("x_in", [T, D])
    wqT = b.din("wqT", [D, D])
    wkT = b.din("wkT", [D, D])
    wvT = b.din("wvT", [D, D])
    woT = b.din("woT", [D, D])
    bo = b.din("bo", [1, D])
    ones_f = b.din("ones_f", [1, 128])
    ones_b = b.din("ones_b", [1, 128], BF16)
    ident = b.din("ident", [128, 128])
    x_out = b.dout("x_out", [T, D])
    emit_l0attn(b, x_in, x_out, wqT, wkT, wvT, woT, bo, ones_f, ones_b, ident, None)
    b.es.close()
    return nc


def emit_l0attn(b, x_in, x_out, wqT, wkT, wvT, woT, bo, ones_f, ones_b, ident, entry):
    nc = b.nc
    emit_barrier(nc, entry, b)
    hln_d = b.dint("hln_d", [T, D])
    q_d = b.dint("q_d", [HEADS, 64, T])
    k_d = b.dint("k_d", [HEADS, 64, T])
    v_d = b.dint("v_d", [T, D])
    wq_b = b.dint("wq_b", [D, D])
    wk_b = b.dint("wk_b", [D, D])
    wv_b = b.dint("wv_b", [D, D])
    wo_b = b.dint("wo_b", [D, D])

    cast = b.sem("cast")
    ld1 = b.sem("ld1")
    dv1 = b.sem("dv1")
    ac1 = b.sem("ac1")
    st1 = b.sem("st1")
    ld2 = b.sem("ld2")
    ldw2 = ld2
    pe2 = b.sem("pe2")
    dv2 = b.sem("dv2")
    st2 = b.sem("st2")
    ld3 = b.sem("ld3")
    pe3 = b.sem("pe3")
    dv3 = b.sem("dv3")
    ac3 = b.sem("ac3")
    ld4 = b.sem("ld4")
    ldx4 = ld4
    pe4 = b.sem("pe4")
    dv4 = b.sem("dv4")
    st4 = b.sem("st4")

    op_sb = [b.sb(f"op_sb{p}", [128, T], BF16) for p in range(8)]
    mxall = b.sb("mxall", [128, 2], F32)

    es1 = contextlib.ExitStack()
    sb1 = lambda n, s, d: es1.enter_context(nc.sbuf_tensor(b.pfx + n, s, d))
    xt = [sb1(f"xt{p}", [128, D], F32) for p in range(2)]
    ht = [sb1(f"ht{p}", [128, D], BF16) for p in range(2)]
    sqs = sb1("sqs", [128, D], F32)
    stat = sb1("stat", [128, 8], F32)
    DV1 = 7
    AC1 = 2

    with nc.Block() as blk:
        @blk.gpsimd
        def _(gpsimd):
            for src, dst in ((wqT, wq_b), (wkT, wk_b), (wvT, wv_b), (woT, wo_b)):
                gpsimd.dma_start(out=dst[:], in_=src[:]).then_inc(cast.h, 16)
        cast.n = 64

        @blk.sync
        def _(sync):
            for i in range(NT):
                if i >= 2:
                    sync.wait_ge(dv1.h, (i - 1) * DV1)
                sync.dma_start(out=xt[i % 2][:], in_=x_in[i * 128:(i + 1) * 128, :]).then_inc(ld1.h, 16)
        ld1.n = 16 * NT

        @blk.scalar
        def _(scalar):
            for i in range(NT):
                base = i * DV1
                scalar.wait_ge(ld1.h, 16 * (i + 1))
                if i >= 1:
                    scalar.wait_ge(dv1.h, i * DV1)
                scalar.activation(sqs[:], xt[i % 2][:], AF.Square,
                                  accum_out=stat[:, 1:2]).then_inc(ac1.h, 1)
                scalar.wait_ge(dv1.h, base + 5)
                scalar.activation(stat[:, 6:7], stat[:, 5:6], AF.Sqrt).then_inc(ac1.h, 1)
        ac1.n = AC1 * NT

        @blk.vector
        def _(vector):
            for i in range(NT):
                abase = i * AC1
                vector.wait_ge(ld1.h, 16 * (i + 1))
                vector.tensor_reduce(stat[:, 0:1], xt[i % 2][:], axis=AX.X, op=ALU.add).then_inc(dv1.h, 1)
                dv1.n += 1
                vector.wait_ge(dv1.h, dv1.n)
                vector.tensor_scalar_mul(stat[:, 2:3], stat[:, 0:1], -1.0 / D).then_inc(dv1.h, 1)
                dv1.n += 1
                vector.wait_ge(dv1.h, dv1.n)
                vector.tensor_tensor(stat[:, 3:4], stat[:, 2:3], stat[:, 2:3], ALU.mult).then_inc(dv1.h, 1)
                dv1.n += 1
                vector.wait_ge(ac1.h, abase + 1)
                vector.wait_ge(dv1.h, dv1.n)
                vector.tensor_scalar(stat[:, 4:5], stat[:, 1:2], 1.0 / D, EPS_LN,
                                     ALU.mult, ALU.add).then_inc(dv1.h, 1)
                dv1.n += 1
                vector.wait_ge(dv1.h, dv1.n)
                vector.tensor_tensor(stat[:, 5:6], stat[:, 4:5], stat[:, 3:4], ALU.subtract).then_inc(dv1.h, 1)
                dv1.n += 1
                vector.wait_ge(ac1.h, abase + 2)
                vector.reciprocal(stat[:, 7:8], stat[:, 6:7]).then_inc(dv1.h, 1)
                dv1.n += 1
                vector.wait_ge(dv1.h, dv1.n)
                if i >= 2:
                    vector.wait_ge(st1.h, 16 * (i - 1))
                vector.tensor_scalar(ht[i % 2][:], xt[i % 2][:], stat[:, 2:3], stat[:, 7:8],
                                     ALU.add, ALU.mult).then_inc(dv1.h, 1)
                dv1.n += 1
            assert dv1.n == DV1 * NT

        @blk.gpsimd
        def _(gpsimd):
            for i in range(NT):
                gpsimd.wait_ge(dv1.h, DV1 * (i + 1))
                gpsimd.dma_start(out=hln_d[i * 128:(i + 1) * 128, :], in_=ht[i % 2][:]).then_inc(st1.h, 16)
        st1.n = 16 * NT

    es1.close()

    es2 = contextlib.ExitStack()
    sb2 = lambda n, s, d: es2.enter_context(nc.sbuf_tensor(b.pfx + n, s, d))
    ps2 = lambda n, s: es2.enter_context(nc.psum_tensor(b.pfx + n, s, F32))
    wqk = [[sb2(f"wqk{h}_{k}", [128, 128], BF16) for k in range(8)] for h in range(HEADS)]
    wvr = [[sb2(f"wvr{k}_{c}", [128, 512], BF16) for c in range(2)] for k in range(8)]
    hlnT = [[sb2(f"hlnT{p}_{k}", [128, 512], BF16) for k in range(8)] for p in range(2)]
    psq = [ps2(f"psq{h}", [128, 512]) for h in range(8)]
    qkb = [sb2(f"qkb{p}", [128, 512], BF16) for p in range(2)]
    vb = [sb2(f"vb{p}", [128, 512], BF16) for p in range(2)]

    pe2_qk = {}
    pe2_v = {}
    c = 0
    for tb in range(8):
        for hg in range(2):
            c += 1
            pe2_qk[(tb, hg)] = c
        for vc in range(2):
            c += 1
            pe2_v[(tb, vc)] = c
    PE2_TOT = c
    dv2_qk = {}
    dv2_v = {}
    c = 1
    for tb in range(8):
        for hg in range(2):
            for h in range(8):
                c += 3
                dv2_qk[(tb, hg, h)] = c
        for vc in range(2):
            for tsub in range(4):
                c += 1
                dv2_v[(tb, vc, tsub)] = c
    DV2_TOT = c
    st2_qk = {}
    st2_v = {}
    c = 0
    for tb in range(8):
        for hg in range(2):
            for h in range(8):
                c += 32
                st2_qk[(tb, hg, h)] = c
        for vc in range(2):
            for tsub in range(4):
                c += 16
                st2_v[(tb, vc, tsub)] = c
    ST2_TOT = c

    with nc.Block() as blk:
        @blk.sync
        def _(sync):
            sync.wait_ge(cast.h, 48)
            for h in range(HEADS):
                for k in range(8):
                    sync.dma_start(out=wqk[h][k][:, 0:64],
                                   in_=wq_b[k * 128:(k + 1) * 128, h * 64:(h + 1) * 64]).then_inc(ldw2.h, 16)
                    sync.dma_start(out=wqk[h][k][:, 64:128],
                                   in_=wk_b[k * 128:(k + 1) * 128, h * 64:(h + 1) * 64]).then_inc(ldw2.h, 16)
            for k in range(8):
                for c_ in range(2):
                    sync.dma_start(out=wvr[k][c_][:],
                                   in_=wv_b[k * 128:(k + 1) * 128, c_ * 512:(c_ + 1) * 512]).then_inc(ldw2.h, 16)
            for tb in range(8):
                sync.wait_ge(st1.h, 16 * min(4 * tb + 4, NT))
                if tb >= 2:
                    sync.wait_ge(pe2.h, pe2_v[(tb - 2, 1)])
                for k in range(8):
                    sync.dma_start(out=hlnT[tb % 2][k][:],
                                   in_=hln_d[tb * 512:(tb + 1) * 512, k * 128:(k + 1) * 128],
                                   transpose=True).then_inc(ld2.h, 16)
        ld2.n = 16 * (HEADS * 8 * 2 + 16 + 64)

        @blk.tensor
        def _(tensor):
            tensor.wait_ge(ldw2.h, 16 * 272)
            for tb in range(8):
                tensor.wait_ge(ld2.h, 16 * (272 + 8 * (tb + 1)))
                for hg in range(2):
                    if hg == 1:
                        tensor.wait_ge(dv2.h, dv2_qk[(tb, 0, 7)])
                    elif tb > 0:
                        tensor.wait_ge(dv2.h, dv2_v[(tb - 1, 1, 3)])
                    mm = None
                    for k in range(8):
                        for h in range(8):
                            mm = tensor.matmul(psq[h][:], wqk[hg * 8 + h][k][:],
                                               hlnT[tb % 2][k][:], start=(k == 0), stop=(k == 7))
                    mm.then_inc(pe2.h, 1)
                for vc in range(2):
                    if vc == 0:
                        tensor.wait_ge(dv2.h, dv2_qk[(tb, 1, 3)])
                    else:
                        tensor.wait_ge(dv2.h, dv2_v[(tb, 0, 3)])
                    mm = None
                    for k in range(8):
                        for tsub in range(4):
                            mm = tensor.matmul(psq[tsub][:],
                                               hlnT[tb % 2][k][:, tsub * 128:(tsub + 1) * 128],
                                               wvr[k][vc][:], start=(k == 0), stop=(k == 7))
                    mm.then_inc(pe2.h, 1)

        @blk.vector
        def _(vector):
            vector.memset(mxall[:, 0:1], 0.0).then_inc(dv2.h, 1)
            dv2.n += 1
            for tb in range(8):
                for hg in range(2):
                    vector.wait_ge(pe2.h, pe2_qk[(tb, hg)])
                    for h in range(8):
                        g = (tb * 2 + hg) * 8 + h
                        if g >= 2:
                            pg_ = g - 2
                            ptb, prem = divmod(pg_, 16)
                            phg, ph = divmod(prem, 8)
                            vector.wait_ge(st2.h, st2_qk[(ptb, phg, ph)])
                        vector.tensor_copy(qkb[g % 2][:], psq[h][:]).then_inc(dv2.h, 1)
                        dv2.n += 1
                        vector.wait_ge(dv2.h, dv2.n)
                        vector.reduce_max(mxall[:, 1:2], qkb[g % 2][:], axis=AX.X,
                                          apply_absolute_value=True).then_inc(dv2.h, 1)
                        dv2.n += 1
                        vector.wait_ge(dv2.h, dv2.n)
                        vector.tensor_tensor(mxall[:, 0:1], mxall[:, 0:1], mxall[:, 1:2],
                                             ALU.max).then_inc(dv2.h, 1)
                        dv2.n += 1
                        vector.wait_ge(dv2.h, dv2.n)
                for vc in range(2):
                    vector.wait_ge(pe2.h, pe2_v[(tb, vc)])
                    for tsub in range(4):
                        g = (tb * 2 + vc) * 4 + tsub
                        if g >= 2:
                            pg_ = g - 2
                            pvtb, prem = divmod(pg_, 8)
                            pvc, pts = divmod(prem, 4)
                            vector.wait_ge(st2.h, st2_v[(pvtb, pvc, pts)])
                        vector.tensor_copy(vb[g % 2][:], psq[tsub][:]).then_inc(dv2.h, 1)
                        dv2.n += 1
            assert dv2.n == DV2_TOT, (dv2.n, DV2_TOT)

        @blk.gpsimd
        def _(gpsimd):
            for tb in range(8):
                for hg in range(2):
                    for h in range(8):
                        gpsimd.wait_ge(dv2.h, dv2_qk[(tb, hg, h)] - 2)
                        hh = hg * 8 + h
                        g = (tb * 2 + hg) * 8 + h
                        gpsimd.dma_start(out=q_d[hh, :, tb * 512:(tb + 1) * 512],
                                         in_=qkb[g % 2][0:64, :]).then_inc(st2.h, 16)
                        gpsimd.dma_start(out=k_d[hh, :, tb * 512:(tb + 1) * 512],
                                         in_=qkb[g % 2][64:128, :]).then_inc(st2.h, 16)
                for vc in range(2):
                    for tsub in range(4):
                        gpsimd.wait_ge(dv2.h, dv2_v[(tb, vc, tsub)])
                        g = (tb * 2 + vc) * 4 + tsub
                        r0 = (tb * 4 + tsub) * 128
                        gpsimd.dma_start(out=v_d[r0:r0 + 128, vc * 512:(vc + 1) * 512],
                                         in_=vb[g % 2][:]).then_inc(st2.h, 16)
        st2.n = ST2_TOT

    es2.close()

    es3 = contextlib.ExitStack()
    sb3 = lambda n, s, d: es3.enter_context(nc.sbuf_tensor(b.pfx + n, s, d))
    ps3 = lambda n, s: es3.enter_context(nc.psum_tensor(b.pfx + n, s, F32))
    qsb = [sb3(f"qsb{p}", [64, T], BF16) for p in range(2)]
    ksb = [sb3(f"ksb{p}", [64, T], BF16) for p in range(2)]
    va = [sb3(f"va{p}", [128, 32 * 65], BF16) for p in range(2)]
    pT = [sb3(f"pT{p}", [128, 32 * 512], BF16) for p in range(2)]
    id_sb = sb3("id_sb", [128, 128], F32)
    onesf_sb = sb3("onesf_sb", [1, 128], F32)
    cst = sb3("cst", [1, 4], F32)
    negc_col = sb3("negc_col", [128, 1], F32)
    rsum = sb3("rsum", [1, 512], F32)
    oub = [sb3(f"oub{p}", [64, 512], F32) for p in range(2)]
    ps_s = [ps3(f"ps_s{i}", [128, 1024]) for i in range(2)]
    ps_o = [ps3(f"ps_o{i}", [65, 512]) for i in range(2)]
    ps_b = ps3("ps_b", [64, 512])
    ps_nc = ps_s[1]
    pmx = ps_s[0]

    DV3_G = 2
    ac3E = {}
    ac3C = {}
    cA = 0
    for g in range(128):
        for grp in range(16):
            cA += 1; ac3E[(g, grp)] = cA
        if g >= 1:
            cA += 1; ac3C[g - 1] = cA
    cA += 1; ac3C[127] = cA
    AC3_END = cA
    pe3S = {}
    pe3AV = {}
    pe3BC = {}
    c3 = 2
    for g in range(129):
        if g < 128:
            for kc in range(32):
                c3 += 1; pe3S[(g, kc)] = c3
        if g >= 1:
            c3 += 1; pe3AV[g - 1] = c3
            c3 += 1; pe3BC[g - 1] = c3
    PE3_END = c3

    def ac3_cnt(h, qb, i):
        return (h * 8 + qb) * AC3_G + i + 1

    def dv3_cnt(h, qb, i):
        return 6 + (h * 8 + qb) * DV3_G + i + 1

    with nc.Block() as blk:
        @blk.sync
        def _(sync):
            sync.dma_start(out=id_sb[:], in_=ident[:]).then_inc(ld3.h, 16)
            sync.dma_start(out=onesf_sb[:], in_=ones_f[:]).then_inc(ld3.h, 16)
            sync.wait_ge(st2.h, ST2_TOT)
            for h in range(HEADS):
                if h >= 2:
                    sync.wait_ge(pe3.h, pe3AV[(h - 2) * 8 + 7])
                sync.dma_start(out=qsb[h % 2][:], in_=q_d[h]).then_inc(ld3.h, 16)
                sync.dma_start(out=ksb[h % 2][:], in_=k_d[h]).then_inc(ld3.h, 16)
                sync.dma_start(out=va[h % 2][:].rearrange("p (kc c) -> p kc c", c=65)[:, :, 0:64],
                               in_=v_d[:, h * 64:(h + 1) * 64].rearrange("(kc p) c -> p kc c", p=128),
                               ).then_inc(ld3.h, 16)
        ld3.n = 16 * (2 + HEADS * 3)

        @blk.vector
        def _(vector):
            for p in range(2):
                vector.memset(va[p][:].rearrange("p (kc c) -> p kc c", c=65)[:, :, 64:65],
                              1.0).then_inc(dv3.h, 1)
                dv3.n += 1
            vector.wait_ge(pe3.h, 1)
            vector.reduce_max(cst[0:1, 0:1], pmx[0:1, 0:128], axis=AX.X).then_inc(dv3.h, 1)
            dv3.n += 1
            vector.wait_ge(dv3.h, dv3.n)
            vector.tensor_tensor(cst[0:1, 1:2], cst[0:1, 0:1], cst[0:1, 0:1], ALU.mult).then_inc(dv3.h, 1)
            dv3.n += 1
            vector.wait_ge(dv3.h, dv3.n)
            vector.tensor_scalar_mul(cst[0:1, 2:3], cst[0:1, 1:2], -8.0).then_inc(dv3.h, 1)
            dv3.n += 1
            vector.wait_ge(pe3.h, 2)
            vector.tensor_copy(negc_col[:], ps_nc[:, 0:1]).then_inc(dv3.h, 1)
            dv3.n += 1
            assert dv3.n == 6
            for h in range(HEADS):
                for qb in range(8):
                    g = h * 8 + qb
                    vector.wait_ge(pe3.h, pe3AV[g])
                    vector.reciprocal(rsum[:], ps_o[g % 2][64:65, :]).then_inc(dv3.h, 1)
                    dv3.n += 1
                    vector.wait_ge(pe3.h, pe3BC[g])
                    vector.wait_ge(ac3.h, ac3C[g])
                    vector.tensor_tensor(op_sb[h // 2][(h % 2) * 64:(h % 2) * 64 + 64,
                                                       qb * 512:(qb + 1) * 512],
                                         oub[g % 2][:], ps_b[:], ALU.mult).then_inc(dv3.h, 1)
                    dv3.n += 1
            assert dv3.n == 6 + 128 * DV3_G

        @blk.tensor
        def _(tensor):
            tensor.wait_ge(dv2.h, DV2_TOT)
            tensor.wait_ge(ld3.h, 32)
            tensor.transpose(pmx[0:1, 0:128], mxall[:, 0:1], id_sb[:]).then_inc(pe3.h, 1)
            tensor.wait_ge(dv3.h, 5)
            tensor.matmul(ps_nc[:, 0:1], onesf_sb[:], cst[0:1, 2:3], start=True,
                          stop=True).then_inc(pe3.h, 1)
            tensor.wait_ge(dv3.h, 6)
            for g in range(129):
                if g < 128:
                    h, qb = divmod(g, 8)
                    if qb == 0:
                        tensor.wait_ge(ld3.h, 16 * (2 + (h + 1) * 3))
                    for kc in range(32):
                        gi = g * 32 + kc
                        grp = gi // 2
                        if grp >= 2:
                            pgrp = grp - 2
                            pg2, pgr = divmod(pgrp, 16)
                            tensor.wait_ge(ac3.h, ac3E[(pg2, pgr)])
                        tensor.matmul(ps_s[(kc // 2) % 2][:, (kc % 2) * 512:(kc % 2) * 512 + 512],
                                      ksb[h % 2][:, kc * 128:(kc + 1) * 128],
                                      qsb[h % 2][:, qb * 512:(qb + 1) * 512],
                                      start=True, stop=True).then_inc(pe3.h, 1)
                if g >= 1:
                    ga = g - 1
                    ha, qba = divmod(ga, 8)
                    tensor.wait_ge(ac3.h, ac3E[(ga, 15)])
                    if ga >= 2:
                        ph2, pqb2 = divmod(ga - 2, 8)
                        tensor.wait_ge(dv3.h, dv3_cnt(ph2, pqb2, 1))
                    mm = None
                    for kc in range(32):
                        mm = tensor.matmul(ps_o[ga % 2][:],
                                           va[ha % 2][:, kc * 65:kc * 65 + 65],
                                           pT[ga % 2][:, kc * 512:(kc + 1) * 512],
                                           start=(kc == 0), stop=(kc == 31))
                    mm.then_inc(pe3.h, 1)
                    tensor.wait_ge(dv3.h, dv3_cnt(ha, qba, 0))
                    if ga >= 1:
                        ph2, pqb2 = divmod(ga - 1, 8)
                        tensor.wait_ge(dv3.h, dv3_cnt(ph2, pqb2, 1))
                    tensor.matmul(ps_b[:], onesf_sb[0:1, 0:64], rsum[:],
                                  start=True, stop=True).then_inc(pe3.h, 1)
            pe3.n = PE3_END

        @blk.scalar
        def _(scalar):
            scalar.wait_ge(dv3.h, 6)
            for g in range(128):
                for grp in range(16):
                    if grp == 0 and g >= 2:
                        scalar.wait_ge(pe3.h, pe3AV[g - 2])
                    scalar.wait_ge(pe3.h, pe3S[(g, 2 * grp + 1)])
                    scalar.activation(pT[g % 2][:, grp * 1024:(grp + 1) * 1024], ps_s[grp % 2][:],
                                      AF.Exp, bias=negc_col[:], scale=0.125).then_inc(ac3.h, 1)
                if g >= 1:
                    ga = g - 1
                    scalar.wait_ge(pe3.h, pe3AV[ga])
                    if ga >= 2:
                        ph2, pqb2 = divmod(ga - 2, 8)
                        scalar.wait_ge(dv3.h, dv3_cnt(ph2, pqb2, 1))
                    scalar.copy(oub[ga % 2][:], ps_o[ga % 2][0:64, :]).then_inc(ac3.h, 1)
            scalar.wait_ge(pe3.h, pe3AV[127])
            scalar.wait_ge(dv3.h, dv3_cnt(15, 5, 1))
            scalar.copy(oub[127 % 2][:], ps_o[127 % 2][0:64, :]).then_inc(ac3.h, 1)
            ac3.n = AC3_END

    es3.close()

    es4 = contextlib.ExitStack()
    sb4 = lambda n, s, d: es4.enter_context(nc.sbuf_tensor(b.pfx + n, s, d))
    ps4 = lambda n, s: es4.enter_context(nc.psum_tensor(b.pfx + n, s, F32))
    wo_sb = [[sb4(f"wo_sb{p}_{f}", [128, 512], BF16) for f in range(2)] for p in range(8)]
    bo_f = sb4("bo_f", [1, D], F32)
    bo_b = sb4("bo_b", [1, D], BF16)
    onesb_sb = sb4("onesb_sb", [1, 128], BF16)
    xr = [sb4(f"xr{p}", [128, D], F32) for p in range(2)]
    xo = [sb4(f"xo{p}", [128, D], F32) for p in range(2)]
    po = [ps4(f"po{i}", [128, 512]) for i in range(4)]

    DV3_TOT = 6 + 128 * DV3_G

    with nc.Block() as blk:
        @blk.sync
        def _(sync):
            sync.wait_ge(cast.h, 64)
            for p in range(8):
                for f in range(2):
                    sync.dma_start(out=wo_sb[p][f][:],
                                   in_=wo_b[p * 128:(p + 1) * 128, f * 512:(f + 1) * 512]).then_inc(ld4.h, 16)
            sync.dma_start(out=bo_f[:], in_=bo[:]).then_inc(ld4.h, 16)
            sync.dma_start(out=onesb_sb[:], in_=ones_b[:]).then_inc(ld4.h, 16)
            for tt in range(NT):
                if tt >= 2:
                    sync.wait_ge(dv4.h, 1 + (tt - 2) * 2 + 2)
                sync.dma_start(out=xr[tt % 2][:], in_=x_in[tt * 128:(tt + 1) * 128, :]).then_inc(ldx4.h, 16)
        ld4.n = 16 * (18 + NT)

        @blk.vector
        def _(vector):
            vector.wait_ge(ld4.h, 16 * 17)
            vector.tensor_copy(bo_b[:], bo_f[:]).then_inc(dv4.h, 1)
            dv4.n += 1
            for tt in range(NT):
                for fc in range(2):
                    g = tt * 2 + fc
                    vector.wait_ge(pe4.h, g + 1)
                    vector.wait_ge(ldx4.h, 16 * (18 + tt + 1))
                    if tt >= 2 and fc == 0:
                        vector.wait_ge(st4.h, 16 * (tt - 1))
                    vector.tensor_add(xo[tt % 2][:, fc * 512:(fc + 1) * 512], po[g % 4][:],
                                      xr[tt % 2][:, fc * 512:(fc + 1) * 512]).then_inc(dv4.h, 1)
                    dv4.n += 1
            assert dv4.n == 1 + 64

        @blk.tensor
        def _(tensor):
            tensor.wait_ge(ld4.h, 16 * 18)
            tensor.wait_ge(dv3.h, DV3_TOT)
            tensor.wait_ge(dv4.h, 1)
            for tt in range(NT):
                for fc in range(2):
                    g = tt * 2 + fc
                    if g >= 4:
                        tensor.wait_ge(dv4.h, 1 + (g - 4) + 1)
                    mm = None
                    for p in range(8):
                        mm = tensor.matmul(po[g % 4][:], op_sb[p][:, tt * 128:(tt + 1) * 128],
                                           wo_sb[p][fc][:], start=(p == 0), stop=False)
                    mm = tensor.matmul(po[g % 4][:], onesb_sb[:], bo_b[0:1, fc * 512:(fc + 1) * 512],
                                       start=False, stop=True)
                    mm.then_inc(pe4.h, 1)
            pe4.n = 64

        @blk.gpsimd
        def _(gpsimd):
            for tt in range(NT):
                gpsimd.wait_ge(dv4.h, 1 + tt * 2 + 2)
                gpsimd.dma_start(out=x_out[tt * 128:(tt + 1) * 128, :], in_=xo[tt % 2][:]).then_inc(st4.h, 16)
            gpsimd.wait_ge(st4.h, 16 * NT)
        st4.n = 16 * NT

    es4.close()
    return (st4.h, st4.n)



def build_moe():
    b = B("moe")
    nc = b.nc
    x_in = b.din("x_in", [T, D])
    wgT = b.din("wgT", [D, E])
    w1s = b.din("w1s", [E, D, HM])
    w2s = b.din("w2s", [E, D, HM])
    w3s = b.din("w3s", [E, HM, D])
    ones_f = b.din("ones_f", [1, 128])
    ident = b.din("ident", [128, 128])
    x_out = b.dout("x_out", [T, D])
    emit_moe(b, x_in, x_out, wgT, w1s, w2s, w3s, ones_f, ident, None)
    b.es.close()
    return nc


def emit_moe(b, x_in, x_out, wgT, w1s, w2s, w3s, ones_f, ident, entry, wb_shared=None):
    nc = b.nc
    emit_barrier(nc, entry, b)
    hln_d = b.dint("hln_d", [T, D])
    y_d = b.dint("y_d", [T, D], F32)
    we_d = b.dint("we_d", [E, T], F32)
    if wb_shared is None:
        w1_b = b.dint("w1_b", [E, D, HM])
        w2_b = b.dint("w2_b", [E, D, HM])
        w3_b = b.dint("w3_b", [E, HM, D])
        do_cast = True
        cast = b.sem("cast")
    else:
        w1_b, w2_b, w3_b, cast = wb_shared
        do_cast = False
    ld1 = b.sem("ld1")
    dv1 = b.sem("dv1")
    ac1 = b.sem("ac1")
    st1 = b.sem("st1")
    ldh = b.sem("ldh")
    stg = b.sem("stg")
    ldwr = b.sem("ldwr")
    ldg = ldh
    ldf = ldwr
    peg = b.sem("peg")
    dvg = b.sem("dvg")
    acg = b.sem("acg")
    ldw = b.sem("ldw")
    ldw3 = ldw
    pex = b.sem("pex")
    dvx = b.sem("dvx")
    acx = b.sem("acx")
    stx = b.sem("stx")
    dvf = b.sem("dvf")
    stf = b.sem("stf")

    hlnT = [[b.sb(f"hlnT{k}_{tb}", [128, 512], BF16) for tb in range(8)] for k in range(8)]

    es1 = contextlib.ExitStack()
    sb1 = lambda n, s, d: es1.enter_context(nc.sbuf_tensor(b.pfx + n, s, d))
    xt = [sb1(f"xt{p}", [128, D], F32) for p in range(2)]
    ht = [sb1(f"ht{p}", [128, D], BF16) for p in range(2)]
    sqs = sb1("sqs", [128, D], F32)
    stat = sb1("stat", [128, 8], F32)
    DV1 = 7
    AC1 = 2

    with nc.Block() as blk:
        if do_cast:
            @blk.gpsimd
            def _(gpsimd):
                for e in range(E):
                    gpsimd.dma_start(out=w1_b[e], in_=w1s[e]).then_inc(cast.h, 16)
                    gpsimd.dma_start(out=w2_b[e], in_=w2s[e]).then_inc(cast.h, 16)
                    gpsimd.dma_start(out=w3_b[e], in_=w3s[e]).then_inc(cast.h, 16)
            cast.n = 16 * 3 * E

        @blk.sync
        def _(sync):
            for i in range(NT):
                if i >= 2:
                    sync.wait_ge(dv1.h, (i - 1) * DV1)
                sync.dma_start(out=xt[i % 2][:], in_=x_in[i * 128:(i + 1) * 128, :]).then_inc(ld1.h, 16)
            for tb in range(8):
                sync.wait_ge(st1.h, 16 * min(4 * tb + 4, NT))
                for k in range(8):
                    sync.dma_start(out=hlnT[k][tb][:],
                                   in_=hln_d[tb * 512:(tb + 1) * 512, k * 128:(k + 1) * 128],
                                   transpose=True).then_inc(ldh.h, 16)
        ld1.n = 16 * NT
        ldh.n = 16 * 64

        @blk.scalar
        def _(scalar):
            for i in range(NT):
                base = i * DV1
                scalar.wait_ge(ld1.h, 16 * (i + 1))
                if i >= 1:
                    scalar.wait_ge(dv1.h, i * DV1)
                scalar.activation(sqs[:], xt[i % 2][:], AF.Square,
                                  accum_out=stat[:, 1:2]).then_inc(ac1.h, 1)
                scalar.wait_ge(dv1.h, base + 5)
                scalar.activation(stat[:, 6:7], stat[:, 5:6], AF.Sqrt).then_inc(ac1.h, 1)
        ac1.n = AC1 * NT

        @blk.vector
        def _(vector):
            for i in range(NT):
                abase = i * AC1
                vector.wait_ge(ld1.h, 16 * (i + 1))
                vector.tensor_reduce(stat[:, 0:1], xt[i % 2][:], axis=AX.X, op=ALU.add).then_inc(dv1.h, 1)
                dv1.n += 1
                vector.wait_ge(dv1.h, dv1.n)
                vector.tensor_scalar_mul(stat[:, 2:3], stat[:, 0:1], -1.0 / D).then_inc(dv1.h, 1)
                dv1.n += 1
                vector.wait_ge(dv1.h, dv1.n)
                vector.tensor_tensor(stat[:, 3:4], stat[:, 2:3], stat[:, 2:3], ALU.mult).then_inc(dv1.h, 1)
                dv1.n += 1
                vector.wait_ge(ac1.h, abase + 1)
                vector.wait_ge(dv1.h, dv1.n)
                vector.tensor_scalar(stat[:, 4:5], stat[:, 1:2], 1.0 / D, EPS_LN,
                                     ALU.mult, ALU.add).then_inc(dv1.h, 1)
                dv1.n += 1
                vector.wait_ge(dv1.h, dv1.n)
                vector.tensor_tensor(stat[:, 5:6], stat[:, 4:5], stat[:, 3:4], ALU.subtract).then_inc(dv1.h, 1)
                dv1.n += 1
                vector.wait_ge(ac1.h, abase + 2)
                vector.reciprocal(stat[:, 7:8], stat[:, 6:7]).then_inc(dv1.h, 1)
                dv1.n += 1
                vector.wait_ge(dv1.h, dv1.n)
                if i >= 2:
                    vector.wait_ge(st1.h, 16 * (i - 1))
                vector.tensor_scalar(ht[i % 2][:], xt[i % 2][:], stat[:, 2:3], stat[:, 7:8],
                                     ALU.add, ALU.mult).then_inc(dv1.h, 1)
                dv1.n += 1
            assert dv1.n == DV1 * NT

        @blk.gpsimd
        def _(gpsimd):
            for i in range(NT):
                gpsimd.wait_ge(dv1.h, DV1 * (i + 1))
                gpsimd.dma_start(out=hln_d[i * 128:(i + 1) * 128, :], in_=ht[i % 2][:]).then_inc(st1.h, 16)
        st1.n = 16 * NT

    es1.close()

    es2 = contextlib.ExitStack()
    sb2 = lambda n, s, d: es2.enter_context(nc.sbuf_tensor(b.pfx + n, s, d))
    ps2 = lambda n, s: es2.enter_context(nc.psum_tensor(b.pfx + n, s, F32))
    wgt_f = sb2("wgt_f", [128, E * 8], F32)
    wgt_sb = sb2("wgt_sb", [128, E * 8], BF16)
    id_sb = sb2("id_sb", [128, 128], F32)
    gsb = sb2("gsb", [128, E], F32)
    gt1 = sb2("gt1", [128, E], F32)
    gw = sb2("gw", [128, E], F32)
    gst = sb2("gst", [128, 6], F32)
    wec = [sb2(f"wec{p}", [8, 128], F32) for p in range(2)]
    pgt = ps2("pgt", [128, E])
    ptr = ps2("ptr", [8, 128])

    GDV = 12
    GAC = 1

    with nc.Block() as blk:
        @blk.sync
        def _(sync):
            sync.dma_start(out=id_sb[:], in_=ident[:]).then_inc(ldg.h, 16)
            for k in range(8):
                sync.dma_start(out=wgt_f[:, k * E:(k + 1) * E],
                               in_=wgT[k * 128:(k + 1) * 128, :]).then_inc(ldg.h, 16)

        @blk.vector
        def _(vector):
            vector.wait_ge(ldg.h, 16 * 73)
            vector.tensor_copy(wgt_sb[:], wgt_f[:]).then_inc(dvg.h, 1)
            dvg.n += 1
            for tt in range(NT):
                vector.wait_ge(peg.h, 2 * tt + 1)
                base = 1 + tt * GDV
                vector.tensor_copy(gsb[:], pgt[:]).then_inc(dvg.h, 1)
                vector.wait_ge(dvg.h, base + 1)
                vector.reduce_max(gst[:, 0:1], gsb[:], axis=AX.X).then_inc(dvg.h, 1)
                vector.wait_ge(dvg.h, base + 2)
                vector.tensor_scalar(gt1[:], gsb[:], gst[:, 0:1], -1e30, ALU.is_ge,
                                     ALU.mult).then_inc(dvg.h, 1)
                vector.wait_ge(dvg.h, base + 3)
                vector.tensor_tensor(gt1[:], gt1[:], gsb[:], ALU.add).then_inc(dvg.h, 1)
                vector.wait_ge(dvg.h, base + 4)
                vector.reduce_max(gst[:, 1:2], gt1[:], axis=AX.X).then_inc(dvg.h, 1)
                vector.wait_ge(dvg.h, base + 5)
                vector.tensor_tensor(gst[:, 2:3], gst[:, 1:2], gst[:, 0:1],
                                     ALU.subtract).then_inc(dvg.h, 1)
                vector.wait_ge(acg.h, tt + 1)
                vector.tensor_scalar(gst[:, 4:5], gst[:, 3:4], -1.0, 1.0, ALU.mult,
                                     ALU.add).then_inc(dvg.h, 1)
                vector.wait_ge(dvg.h, base + 7)
                vector.tensor_single_scalar(gw[:], gsb[:], gst[:, 0:1], ALU.is_ge).then_inc(dvg.h, 1)
                vector.wait_ge(dvg.h, base + 8)
                vector.tensor_single_scalar(gw[:], gw[:], gst[:, 4:5], ALU.mult).then_inc(dvg.h, 1)
                vector.wait_ge(dvg.h, base + 9)
                vector.tensor_single_scalar(gt1[:], gt1[:], gst[:, 1:2], ALU.is_ge).then_inc(dvg.h, 1)
                vector.wait_ge(dvg.h, base + 10)
                vector.scalar_tensor_tensor(gw[:], gt1[:], gst[:, 3:4], gw[:], ALU.mult,
                                            ALU.add).then_inc(dvg.h, 1)
                vector.wait_ge(dvg.h, base + 11)
                vector.wait_ge(peg.h, 2 * tt + 2)
                if tt >= 2:
                    vector.wait_ge(stg.h, 16 * (tt - 1))
                vector.tensor_copy(wec[tt % 2][:], ptr[:]).then_inc(dvg.h, 1)
                dvg.n += GDV
            assert dvg.n == 1 + GDV * NT

        @blk.scalar
        def _(scalar):
            for tt in range(NT):
                scalar.wait_ge(dvg.h, 1 + tt * GDV + 6)
                scalar.activation(gst[:, 3:4], gst[:, 2:3], AF.Sigmoid).then_inc(acg.h, 1)
        acg.n = NT

        @blk.gpsimd
        def _(gpsimd):
            for tt in range(NT):
                gpsimd.wait_ge(dvg.h, 1 + tt * GDV + 12)
                gpsimd.dma_start(out=we_d[:, tt * 128:(tt + 1) * 128],
                                 in_=wec[tt % 2][:]).then_inc(stg.h, 16)
        stg.n = 16 * NT

        @blk.tensor
        def _(tensor):
            tensor.wait_ge(dvg.h, 1)
            for tt in range(NT):
                tb, ts_ = divmod(tt, 4)
                if ts_ == 0:
                    tensor.wait_ge(ldh.h, 16 * 8 * (tb + 1))
                if tt >= 1:
                    tensor.wait_ge(dvg.h, 1 + (tt - 1) * GDV + 1)
                mm = None
                for k in range(8):
                    mm = tensor.matmul(pgt[:], hlnT[k][tb][:, ts_ * 128:(ts_ + 1) * 128],
                                       wgt_sb[:, k * E:(k + 1) * E], start=(k == 0), stop=(k == 7))
                mm.then_inc(peg.h, 1)
                tensor.wait_ge(dvg.h, 1 + tt * GDV + 11)
                if tt >= 1:
                    tensor.wait_ge(dvg.h, 1 + (tt - 1) * GDV + 12)
                tensor.transpose(ptr[:], gw[:], id_sb[:]).then_inc(peg.h, 1)
        peg.n = 2 * NT

    es2.close()

    es3 = contextlib.ExitStack()
    sb3 = lambda n, s, d: es3.enter_context(nc.sbuf_tensor(b.pfx + n, s, d))
    ps3 = lambda n, s: es3.enter_context(nc.psum_tensor(b.pfx + n, s, F32))
    w12f = [[sb3(f"w12f{w}_{k}", [128, HM], BF16) for k in range(8)] for w in range(2)]
    w3f = [sb3(f"w3f{hc}", [128, D], BF16) for hc in range(16)]
    wb_sb = sb3("wb_sb", [128, 512], BF16)
    onesf_sb = sb3("onesf_sb", [1, 128], F32)
    wrow = [sb3(f"wrow{p}", [1, 512], F32) for p in range(2)]
    gsil = [sb3(f"gsil{hc}", [128, 512], BF16) for hc in range(16)]
    ssb = [sb3(f"ssb{i}", [128, 512], F32) for i in range(2)]
    h2w = [sb3(f"h2w{i}", [128, 512], F32) for i in range(2)]
    ysb = [sb3(f"ysb{i}", [128, 512], F32) for i in range(2)]
    ph = [ps3(f"ph{i}", [128, 512]) for i in range(8)]
    pw = ph[5]

    pex_wb = {}
    pex_h1 = {}
    pex_h2 = {}
    pex_y = {}
    dvx_wb = {}
    dvx_hw = {}
    dvx_g = {}
    dvx_y = {}
    acx_s = {}
    stx_y = {}
    pc = dc = ac_ = sc = 0
    for e in range(E):
        for tbo in range(8):
            pc += 1; pex_wb[(e, tbo)] = pc
            dc += 1; dvx_wb[(e, tbo)] = dc
            for hc in range(16):
                pc += 1; pex_h1[(e, tbo, hc)] = pc
                pc += 1; pex_h2[(e, tbo, hc)] = pc
                ac_ += 1; acx_s[(e, tbo, hc)] = ac_
                dc += 1; dvx_hw[(e, tbo, hc)] = dc
                dc += 1; dvx_g[(e, tbo, hc)] = dc
            for i8 in range(8):
                pc += 1; pex_y[(e, tbo, i8)] = pc
                dc += 1; dvx_y[(e, tbo, i8)] = dc
                sc += 16; stx_y[(e, tbo, i8)] = sc
    PEX_TOT, DVX_TOT, ACX_TOT, STX_TOT = pc, dc, ac_, sc

    def prev_tbo(e, tbo):
        if tbo > 0:
            return (e, tbo - 1)
        if e > 0:
            return (e - 1, 7)
        return None

    with nc.Block() as blk:
        @blk.sync
        def _(sync):
            sync.dma_start(out=onesf_sb[:], in_=ones_f[:]).then_inc(ldw.h, 16)
            sync.wait_ge(stg.h, 16 * NT)
            for e in range(E):
                if do_cast:
                    sync.wait_ge(cast.h, 16 * (3 * e + 3))
                elif e == 0:
                    sync.wait_ge(cast.h, cast.n)
                if e >= 1:
                    sync.wait_ge(pex.h, pex_h2[(e - 1, 7, 15)])
                    sync.wait_ge(pex.h, pex_y[(e - 1, 7, 7)])
                for k in range(8):
                    sync.dma_start(out=w12f[0][k][:],
                                   in_=w1_b[e, k * 128:(k + 1) * 128, :]).then_inc(ldw.h, 16)
                    sync.dma_start(out=w12f[1][k][:],
                                   in_=w2_b[e, k * 128:(k + 1) * 128, :]).then_inc(ldw.h, 16)
                for hc in range(16):
                    sync.dma_start(out=w3f[hc][:],
                                   in_=w3_b[e, hc * 128:(hc + 1) * 128, :]).then_inc(ldw.h, 16)
                for tbo in range(8):
                    g = e * 8 + tbo
                    if g >= 2:
                        sync.wait_ge(pex.h, pex_wb[((g - 2) // 8, (g - 2) % 8)])
                    sync.dma_start(out=wrow[g % 2][:],
                                   in_=we_d[e:e + 1, tbo * 512:(tbo + 1) * 512]).then_inc(ldwr.h, 16)
        ldw.n = 16 * (1 + E * 32)
        ldwr.n = 16 * E * 8

        @blk.tensor
        def _(tensor):
            tensor.wait_ge(ldw.h, 16)
            for e in range(E):
                tensor.wait_ge(ldw.h, 16 * (1 + 32 * (e + 1)))
                for tbo in range(8):
                    g = e * 8 + tbo
                    tensor.wait_ge(ldwr.h, 16 * (g + 1))
                    pv = prev_tbo(e, tbo)
                    if pv is not None:
                        tensor.wait_ge(dvx.h, dvx_wb[pv])
                        tensor.wait_ge(dvx.h, dvx_y[(pv[0], pv[1], 5)])
                    tensor.matmul(pw[:], onesf_sb[:], wrow[g % 2][:],
                                  start=True, stop=True).then_inc(pex.h, 1)
                    for hc in range(16):
                        if hc >= 2:
                            tensor.wait_ge(dvx.h, dvx_g[(e, tbo, hc - 2)])
                        elif pv is not None:
                            tensor.wait_ge(dvx.h, dvx_g[(pv[0], pv[1], 14 + hc)])
                        mm = None
                        for k in range(8):
                            mm = tensor.matmul(ph[hc % 2][:],
                                               w12f[0][k][:, hc * 128:(hc + 1) * 128],
                                               hlnT[k][tbo][:], start=(k == 0), stop=(k == 7))
                        mm.then_inc(pex.h, 1)
                        mm = None
                        for k in range(8):
                            mm = tensor.matmul(ph[2 + hc % 2][:],
                                               w12f[1][k][:, hc * 128:(hc + 1) * 128],
                                               hlnT[k][tbo][:], start=(k == 0), stop=(k == 7))
                        mm.then_inc(pex.h, 1)
                    tensor.wait_ge(dvx.h, dvx_g[(e, tbo, 15)])
                    for i8 in range(8):
                        tsub, fc = divmod(i8, 2)
                        bank = 4 + i8 % 4
                        if i8 >= 4:
                            tensor.wait_ge(dvx.h, dvx_y[(e, tbo, i8 - 4)])
                        elif pv is not None:
                            tensor.wait_ge(dvx.h, dvx_y[(pv[0], pv[1], i8 + 4)])
                        if bank == 5:
                            tensor.wait_ge(dvx.h, dvx_wb[(e, tbo)])
                        mm = None
                        for hc in range(16):
                            mm = tensor.matmul(ph[bank][:],
                                               gsil[hc][:, tsub * 128:(tsub + 1) * 128],
                                               w3f[hc][:, fc * 512:(fc + 1) * 512],
                                               start=(hc == 0), stop=(hc == 15))
                        mm.then_inc(pex.h, 1)

        @blk.scalar
        def _(scalar):
            for e in range(E):
                for tbo in range(8):
                    for hc in range(16):
                        scalar.wait_ge(pex.h, pex_h1[(e, tbo, hc)])
                        if hc >= 2:
                            scalar.wait_ge(dvx.h, dvx_g[(e, tbo, hc - 2)])
                        else:
                            pv = prev_tbo(e, tbo)
                            if pv is not None:
                                scalar.wait_ge(dvx.h, dvx_g[(pv[0], pv[1], 14 + hc)])
                        scalar.activation(ssb[hc % 2][:], ph[hc % 2][:], AF.Silu).then_inc(acx.h, 1)
        acx.n = ACX_TOT

        @blk.vector
        def _(vector):
            for e in range(E):
                for tbo in range(8):
                    vector.wait_ge(pex.h, pex_wb[(e, tbo)])
                    vector.tensor_copy(wb_sb[:], pw[:]).then_inc(dvx.h, 1)
                    dvx.n += 1
                    for hc in range(16):
                        vector.wait_ge(acx.h, acx_s[(e, tbo, hc)])
                        vector.wait_ge(pex.h, pex_h2[(e, tbo, hc)])
                        vector.tensor_tensor(h2w[hc % 2][:], ph[2 + hc % 2][:], wb_sb[:],
                                             ALU.mult).then_inc(dvx.h, 1)
                        dvx.n += 1
                        if hc == 0:
                            pv = prev_tbo(e, tbo)
                            if pv is not None:
                                vector.wait_ge(pex.h, pex_y[(pv[0], pv[1], 7)])
                        vector.wait_ge(dvx.h, dvx.n)
                        vector.tensor_tensor(gsil[hc][:], ssb[hc % 2][:], h2w[hc % 2][:],
                                             ALU.mult).then_inc(dvx.h, 1)
                        dvx.n += 1
                    for i8 in range(8):
                        vector.wait_ge(pex.h, pex_y[(e, tbo, i8)])
                        gg = (e * 8 + tbo) * 8 + i8
                        if gg >= 2:
                            vector.wait_ge(stx.h, 16 * (gg - 1))
                        vector.tensor_copy(ysb[i8 % 2][:], ph[4 + i8 % 4][:]).then_inc(dvx.h, 1)
                        dvx.n += 1
            assert dvx.n == DVX_TOT, (dvx.n, DVX_TOT)

        @blk.gpsimd
        def _(gpsimd):
            for e in range(E):
                for tbo in range(8):
                    for i8 in range(8):
                        tsub, fc = divmod(i8, 2)
                        gpsimd.wait_ge(dvx.h, dvx_y[(e, tbo, i8)])
                        r0 = tbo * 512 + tsub * 128
                        if e == 0:
                            gpsimd.dma_start(out=y_d[r0:r0 + 128, fc * 512:(fc + 1) * 512],
                                             in_=ysb[i8 % 2][:]).then_inc(stx.h, 16)
                        else:
                            gpsimd.dma_start(out=y_d[r0:r0 + 128, fc * 512:(fc + 1) * 512],
                                             in_=ysb[i8 % 2][:],
                                             accum_op=ALU.add).then_inc(stx.h, 16)
        stx.n = STX_TOT

    es3.close()

    es4 = contextlib.ExitStack()
    sb4 = lambda n, s, d: es4.enter_context(nc.sbuf_tensor(b.pfx + n, s, d))
    xr = [sb4(f"xr{p}", [128, D], F32) for p in range(2)]
    yr = [sb4(f"yr{p}", [128, D], F32) for p in range(2)]
    xo = [sb4(f"xo{p}", [128, D], F32) for p in range(2)]

    with nc.Block() as blk:
        @blk.sync
        def _(sync):
            sync.wait_ge(stx.h, STX_TOT)
            for i in range(NT):
                if i >= 2:
                    sync.wait_ge(dvf.h, i - 1)
                sync.dma_start(out=xr[i % 2][:], in_=x_in[i * 128:(i + 1) * 128, :]).then_inc(ldf.h, 16)
                sync.dma_start(out=yr[i % 2][:], in_=y_d[i * 128:(i + 1) * 128, :]).then_inc(ldf.h, 16)

        @blk.vector
        def _(vector):
            for i in range(NT):
                vector.wait_ge(ldf.h, 16 * 64 + 32 * (i + 1))
                if i >= 2:
                    vector.wait_ge(stf.h, 16 * (i - 1))
                vector.scalar_tensor_tensor(xo[i % 2][:], yr[i % 2][:], 0.5, xr[i % 2][:],
                                            ALU.mult, ALU.add).then_inc(dvf.h, 1)
                dvf.n += 1

        @blk.gpsimd
        def _(gpsimd):
            for i in range(NT):
                gpsimd.wait_ge(dvf.h, i + 1)
                gpsimd.dma_start(out=x_out[i * 128:(i + 1) * 128, :], in_=xo[i % 2][:]).then_inc(stf.h, 16)
            gpsimd.wait_ge(stf.h, 16 * NT)
        stf.n = 16 * NT

    es4.close()
    return (stf.h, stf.n)



def build_l1attn():
    b = B("l1attn")
    nc = b.nc
    x_in = b.din("x_in", [T, D])
    w_qkv = b.din("w_qkv", [L1DEPTH, D, 3 * D])
    w_out = b.din("w_out", [L1DEPTH, D, D])
    b_out = b.din("b_out", [L1DEPTH, D])
    gamma = b.din("gamma", [L1DEPTH, D])
    beta_pc = b.din("beta_pc", [L1DEPTH, 128, 8])
    ones_f = b.din("ones_f", [1, 128])
    ones_b = b.din("ones_b", [1, 128], BF16)
    ones_b512 = b.din("ones_b512", [1, 512], BF16)
    x_out = b.dout("x_out", [T, D])
    emit_l1attn(b, x_in, x_out, w_qkv, w_out, b_out, gamma, beta_pc,
                ones_f, ones_b, ones_b512, None)
    b.es.close()
    return nc


def emit_l1attn(b, x_in, x_out, w_qkv, w_out, b_out, gamma, beta_pc,
                ones_f, ones_b, ones_b512, entry, wb=None):
    nc = b.nc
    emit_barrier(nc, entry, b)
    hln_d = b.dint("hln_d", [T, D])
    qs_d = b.dint("qs_d", [D, T])
    ks_d = b.dint("ks_d", [T, D])
    vT_d = b.dint("vT_d", [D, T])
    xa_d = b.dint("xa_d", [T, D], F32)
    if wb is None:
        wqkv_b = b.dint("wqkv_b", [L1DEPTH, D, 3 * D])
        wout_b = b.dint("wout_b", [L1DEPTH, D, D])
    else:
        wqkv_b, wout_b, shared_cast = wb

    sm = {n: b.sem(n) for n in
          ["ldm", "ldwq", "ldx", "ldhT", "ldC", "ldD",
           "peS", "dvS", "acS", "stA", "stB", "stE", "ldF", "dvF", "stF"]}
    sm["cast"] = b.sem("cast") if wb is None else shared_cast

    gp = dict(
        hlnT=[[b.sb(f"hlnT{p}_{k}", [128, 512], BF16) for k in range(8)] for p in range(2)],
        gb_sb=b.sb("gb_sb", [128, D], F32),
        grow=b.sb("grow", [1, D], F32),
        bcol_f=b.sb("bcol_f", [128, 8], F32),
        bcol_b=b.sb("bcol_b", [128, 8], BF16),
        brow_f=b.sb("brow_f", [1, D], F32),
        brow_b=b.sb("brow_b", [1, D], BF16),
        bW_b=b.sb("bW_b", [1, 3 * D], BF16),
        onesf_sb=b.sb("onesf_sb", [1, 128], F32),
        onesb_sb=b.sb("onesb_sb", [1, 128], BF16),
        onesb5_sb=b.sb("onesb5_sb", [1, 512], BF16),
        xt=[b.sb(f"xt{p}", [128, D], F32) for p in range(2)],
        ht=[b.sb(f"ht{p}", [128, D], BF16) for p in range(2)],
        sqs=b.sb("sqs", [128, D], F32),
        stat=b.sb("stat", [128, 8], F32),
        kst=b.sb("kst", [128, 8], F32),
        qst=b.sb("qst", [128, 4], F32),
        xoD=[b.sb(f"xoD{p}", [128, D], F32) for p in range(2)],
        pq=[b.ps(f"pq{i}", [128, 512], F32) for i in range(8)],
    )

    with nc.Block() as blk:
        if wb is None:
            @blk.gpsimd
            def _(gpsimd):
                gpsimd.dma_start(out=wqkv_b[:], in_=w_qkv[:]).then_inc(sm["cast"].h, 16)
                gpsimd.dma_start(out=wout_b[:], in_=w_out[:]).then_inc(sm["cast"].h, 16)
            sm["cast"].n = 32

        @blk.sync
        def _(sync):
            sync.dma_start(out=gp["onesf_sb"][:], in_=ones_f[:]).then_inc(sm["ldm"].h, 16)
            sync.dma_start(out=gp["onesb_sb"][:], in_=ones_b[:]).then_inc(sm["ldm"].h, 16)
            sync.dma_start(out=gp["onesb5_sb"][:], in_=ones_b512[:]).then_inc(sm["ldm"].h, 16)
        sm["ldm"].n = 48

    xt, xoD, sqs, stat = gp["xt"], gp["xoD"], gp["sqs"], gp["stat"]
    ldx, dvS, acS, stE = sm["ldx"], sm["dvS"], sm["acS"], sm["stE"]
    DV0 = 7
    dv00 = dvS.n
    ac00 = acS.n
    st00 = stE.n
    ldx00 = ldx.n
    with nc.Block() as blk:
        @blk.sync
        def _(sync):
            for i in range(NT):
                if i >= 2:
                    sync.wait_ge(dvS.h, dv00 + (i - 1) * DV0)
                sync.dma_start(out=xt[i % 2][:], in_=x_in[i * 128:(i + 1) * 128, :]).then_inc(ldx.h, 16)
        ldx.n = ldx00 + 16 * NT

        @blk.scalar
        def _(scalar):
            for i in range(NT):
                base = dv00 + i * DV0
                scalar.wait_ge(ldx.h, ldx00 + 16 * (i + 1))
                if i >= 1:
                    scalar.wait_ge(dvS.h, dv00 + i * DV0)
                scalar.activation(sqs[:], xt[i % 2][:], AF.Square,
                                  accum_out=stat[:, 1:2]).then_inc(acS.h, 1)
                scalar.wait_ge(dvS.h, base + 5)
                scalar.activation(stat[:, 6:7], stat[:, 5:6], AF.Sqrt).then_inc(acS.h, 1)
        acS.n = ac00 + 2 * NT

        @blk.vector
        def _(vector):
            for i in range(NT):
                abase = ac00 + i * 2
                vector.wait_ge(ldx.h, ldx00 + 16 * (i + 1))
                vector.tensor_reduce(stat[:, 0:1], xt[i % 2][:], axis=AX.X, op=ALU.add).then_inc(dvS.h, 1)
                dvS.n += 1
                vector.wait_ge(dvS.h, dvS.n)
                vector.tensor_scalar_mul(stat[:, 2:3], stat[:, 0:1], -1.0 / D).then_inc(dvS.h, 1)
                dvS.n += 1
                vector.wait_ge(dvS.h, dvS.n)
                vector.tensor_tensor(stat[:, 3:4], stat[:, 2:3], stat[:, 2:3], ALU.mult).then_inc(dvS.h, 1)
                dvS.n += 1
                vector.wait_ge(acS.h, abase + 1)
                vector.wait_ge(dvS.h, dvS.n)
                vector.tensor_scalar(stat[:, 4:5], stat[:, 1:2], 1.0 / D, EPS_LN,
                                     ALU.mult, ALU.add).then_inc(dvS.h, 1)
                dvS.n += 1
                vector.wait_ge(dvS.h, dvS.n)
                vector.tensor_tensor(stat[:, 5:6], stat[:, 4:5], stat[:, 3:4], ALU.subtract).then_inc(dvS.h, 1)
                dvS.n += 1
                vector.wait_ge(acS.h, abase + 2)
                vector.reciprocal(stat[:, 7:8], stat[:, 6:7]).then_inc(dvS.h, 1)
                dvS.n += 1
                vector.wait_ge(dvS.h, dvS.n)
                if i >= 2:
                    vector.wait_ge(stE.h, st00 + 16 * (i - 1))
                vector.tensor_scalar(xoD[i % 2][:], xt[i % 2][:], stat[:, 2:3], stat[:, 7:8],
                                     ALU.add, ALU.mult).then_inc(dvS.h, 1)
                dvS.n += 1
        assert dvS.n == dv00 + DV0 * NT

        @blk.gpsimd
        def _(gpsimd):
            for i in range(NT):
                gpsimd.wait_ge(dvS.h, dv00 + DV0 * (i + 1))
                gpsimd.dma_start(out=xa_d[i * 128:(i + 1) * 128, :], in_=xoD[i % 2][:]).then_inc(stE.h, 16)
        stE.n = st00 + 16 * NT

    dr = dict(wqkv_b=wqkv_b, wout_b=wout_b, b_out=b_out, gamma=gamma,
              beta_pc=beta_pc, hln_d=hln_d, qs_d=qs_d, ks_d=ks_d, vT_d=vT_d)
    for l in range(L1DEPTH):
        _emit_l1_sublayer(b, nc, l, xa_d, xa_d, dr, gp, sm)

    ldF, dvF, stF = sm["ldF"], sm["dvF"], sm["stF"]
    xrF = gp["ht"]
    with nc.Block() as blk:
        @blk.sync
        def _(sync):
            sync.wait_ge(stE.h, stE.n)
            for i in range(NT):
                if i >= 2:
                    sync.wait_ge(dvF.h, i - 1)
                sync.dma_start(out=xt[i % 2][:], in_=x_in[i * 128:(i + 1) * 128, :]).then_inc(ldF.h, 16)
                sync.dma_start(out=sqs[:] if False else (gp["gb_sb"][:] if i % 2 == 0 else sqs[:]),
                               in_=xa_d[i * 128:(i + 1) * 128, :]).then_inc(ldF.h, 16)
        ldF.n = 32 * NT

        @blk.vector
        def _(vector):
            for i in range(NT):
                vector.wait_ge(ldF.h, 32 * (i + 1))
                if i >= 2:
                    vector.wait_ge(stF.h, 16 * (i - 1))
                vector.tensor_add(xoD[i % 2][:], xt[i % 2][:],
                                  gp["gb_sb"][:] if i % 2 == 0 else sqs[:]).then_inc(dvF.h, 1)
                dvF.n += 1

        @blk.gpsimd
        def _(gpsimd):
            for i in range(NT):
                gpsimd.wait_ge(dvF.h, i + 1)
                gpsimd.dma_start(out=x_out[i * 128:(i + 1) * 128, :], in_=xoD[i % 2][:]).then_inc(stF.h, 16)
            gpsimd.wait_ge(stF.h, 16 * NT)
        stF.n = 16 * NT

    return (stF.h, stF.n)


def _emit_l1_sublayer(b, nc, l, x_src, x_dst, dr, gp, sm):
    hlnT, gb_sb, grow = gp["hlnT"], gp["gb_sb"], gp["grow"]
    bcol_f, bcol_b, brow_f, brow_b = gp["bcol_f"], gp["bcol_b"], gp["brow_f"], gp["brow_b"]
    bW_b, onesf_sb, onesb_sb, onesb5_sb = gp["bW_b"], gp["onesf_sb"], gp["onesb_sb"], gp["onesb5_sb"]
    xt, ht, sqs, stat = gp["xt"], gp["ht"], gp["sqs"], gp["stat"]
    kst, qst, xoD, pq = gp["kst"], gp["qst"], gp["xoD"], gp["pq"]
    cast, ldm, ldwq = sm["cast"], sm["ldm"], sm["ldwq"]
    ldx, ldhT, ldC, ldD = sm["ldx"], sm["ldhT"], sm["ldC"], sm["ldD"]
    peS, dvS, acS, stA, stB, stE = sm["peS"], sm["dvS"], sm["acS"], sm["stA"], sm["stB"], sm["stE"]
    wqkv_b, wout_b = dr["wqkv_b"], dr["wout_b"]
    b_out, gamma, beta_pc = dr["b_out"], dr["gamma"], dr["beta_pc"]
    hln_d, qs_d, ks_d, vT_d = dr["hln_d"], dr["qs_d"], dr["ks_d"], dr["vT_d"]

    peB0, dvB0, acB0 = peS.n, dvS.n, acS.n
    stA0, stB0, stE0 = stA.n, stB.n, stE.n
    ldx0, ldm0, ldhT0 = ldx.n, ldm.n, ldhT.n
    ldwq0, ldC0, ldD0 = ldwq.n, ldC.n, ldD.n
    barrier = stE0

    pe_gb, pe_bw = {}, {}
    pe_q, pe_k, pe_v = {}, {}, {}
    pe_C, pe_D, pe_E = {}, {}, {}
    c = peB0
    for g in range(2):
        c += 1; pe_gb[g] = c
    for n_ in range(6):
        c += 1; pe_bw[n_] = c
    for tb in range(8):
        for jc in range(8):
            c += 1; pe_q[(tb, jc)] = c
        for tt in range(4):
            c += 1; pe_k[(tb, tt)] = c
        for jc in range(8):
            c += 1; pe_v[(tb, jc)] = c
    for ch in range(2):
        for i in range(32):
            c += 1; pe_C[(ch, i)] = c
    for tb in range(8):
        for cc in range(8):
            c += 1; pe_D[(tb, cc)] = c
        for i8 in range(8):
            c += 1; pe_E[(tb, i8)] = c
    PE_END = c

    dv_pre, dv_A = {}, {}
    dv_qe, dv_qs, dv_k, dv_ve = {}, {}, {}, {}
    dv_C, dv_D, dv_E = {}, {}, {}
    c = dvB0
    for nm in ('bcol', 'brow', 'gb0', 'gb1', 'bw0', 'bw1', 'bw2', 'bw3', 'bw4', 'bw5'):
        c += 1; dv_pre[nm] = c
    for i in range(NT):
        c += 9; dv_A[i] = c
    for tb in range(8):
        for jc in range(8):
            c += 1; dv_qe[(tb, jc)] = c
        for tt in range(4):
            c += 7; dv_k[(tb, tt)] = c
        for jc in range(8):
            c += 1; dv_ve[(tb, jc)] = c
    for jc in range(8):
        c += 4; dv_qs[jc] = c
    for ch in range(2):
        for cp in range(8):
            c += 1; dv_C[(ch, cp)] = c
    for tb in range(8):
        for cc in range(8):
            c += 1; dv_D[(tb, cc)] = c
        for i8 in range(8):
            c += 1; dv_E[(tb, i8)] = c
    DV_END = c

    ac_A, ac_k, ac_q = {}, {}, {}
    c = acB0
    for i in range(NT):
        c += 2; ac_A[i] = c
    for tb in range(8):
        for tt in range(4):
            c += 2; ac_k[(tb, tt)] = c
    for jc in range(8):
        c += 1; ac_q[jc] = c
    AC_END = c

    st_A = {}
    c = stA0
    for i in range(NT):
        c += 16; st_A[i] = c
    ST_A_END = c
    st_k, st_v, st_q = {}, {}, {}
    c = stB0
    for tb in range(8):
        for tt in range(4):
            c += 16; st_k[(tb, tt)] = c
        for jc in range(8):
            c += 16; st_v[(tb, jc)] = c
    for jc in range(8):
        c += 16; st_q[jc] = c
    ST_B_END = c
    st_E = {}
    c = stE0
    for tb in range(8):
        for tsub in range(4):
            c += 16; st_E[(tb, tsub)] = c
    ST_E_END = c

    ld_hT = {}
    c = ldhT0
    for tb in range(8):
        c += 16 * 8; ld_hT[tb] = c
    LD_HT_END = c
    ld_C = {}
    c = ldC0
    for ch in range(2):
        for ie in range(16):
            c += 32; ld_C[(ch, ie)] = c
    LD_C_END = c
    ld_Codd = {}
    c = ldD0
    for ch in range(2):
        for io in range(16):
            c += 32; ld_Codd[(ch, io)] = c
    LD_CODD_END = c
    ld_D = {}
    c = LD_CODD_END
    c += 16 * 16
    LD_D_W = c
    for tb in range(8):
        c += 16 * 12; ld_D[tb] = c
    LD_D_END = c

    esA = contextlib.ExitStack()
    sbA = lambda n, s, d: esA.enter_context(nc.sbuf_tensor(b.pfx + f"s{l}_" + n, s, d))
    wqkv_sb = [[sbA(f"wqkv_sb{k}_{n}", [128, 512], BF16) for n in range(6)] for k in range(8)]

    PES_SUB = 360
    DVS_SUB = 826
    with nc.Block() as blk:
        @blk.sync
        def _(sync):
            sync.wait_ge(peS.h, max(0, peB0 - 358))
            sync.dma_start(out=grow[:], in_=gamma[l:l + 1, :]).then_inc(ldm.h, 16)
            sync.wait_ge(dvS.h, max(0, dvB0 - DVS_SUB + 2))
            sync.dma_start(out=bcol_f[:], in_=beta_pc[l]).then_inc(ldm.h, 16)
            sync.dma_start(out=brow_f[:], in_=b_out[l:l + 1, :]).then_inc(ldm.h, 16)
            sync.wait_ge(cast.h, cast.n)
            sync.wait_ge(peS.h, max(0, peB0 - 192))
            for k in range(8):
                for n_ in range(6):
                    sync.dma_start(out=wqkv_sb[k][n_][:],
                                   in_=wqkv_b[l, k * 128:(k + 1) * 128,
                                              n_ * 512:(n_ + 1) * 512]).then_inc(ldwq.h, 16)
            for i in range(NT):
                if i >= 2:
                    sync.wait_ge(dvS.h, dv_A[i - 1])
                sync.wait_ge(stE.h, max(0, barrier - 512 + 16 * (i + 1)))
                sync.dma_start(out=xt[i % 2][:], in_=x_src[i * 128:(i + 1) * 128, :]).then_inc(ldx.h, 16)
        ldm.n = ldm0 + 48
        ldwq.n = ldwq0 + 16 * 48
        ldx.n = ldx0 + 16 * NT

        @blk.tensor
        def _(tensor):
            tensor.wait_ge(dvS.h, dvB0)
            tensor.wait_ge(ldm.h, ldm0 + 48)
            for g in range(2):
                tensor.matmul(pq[6 + g][:], onesf_sb[:], grow[0:1, g * 512:(g + 1) * 512],
                              start=True, stop=True).then_inc(peS.h, 1)
            tensor.wait_ge(dvS.h, dv_pre['bcol'])
            tensor.wait_ge(ldwq.h, ldwq0 + 16 * 48)
            for n_ in range(6):
                if n_ >= 2:
                    tensor.wait_ge(dvS.h, dv_pre[f'bw{n_ - 2}'])
                mm = None
                for k in range(8):
                    mm = tensor.matmul(pq[4 + n_ % 2][0:1, :], bcol_b[:, k:k + 1],
                                       wqkv_sb[k][n_][:], start=(k == 0), stop=(k == 7))
                mm.then_inc(peS.h, 1)

        @blk.vector
        def _(vector):
            vector.wait_ge(ldm.h, ldm0 + 32)
            vector.tensor_copy(bcol_b[:], bcol_f[:]).then_inc(dvS.h, 1)
            dvS.n += 1
            vector.wait_ge(ldm.h, ldm0 + 48)
            vector.tensor_copy(brow_b[:], brow_f[:]).then_inc(dvS.h, 1)
            dvS.n += 1
            for g in range(2):
                vector.wait_ge(peS.h, pe_gb[g])
                vector.tensor_copy(gb_sb[:, g * 512:(g + 1) * 512], pq[6 + g][:]).then_inc(dvS.h, 1)
                dvS.n += 1
            for n_ in range(6):
                vector.wait_ge(peS.h, pe_bw[n_])
                vector.tensor_copy(bW_b[0:1, n_ * 512:(n_ + 1) * 512],
                                   pq[4 + n_ % 2][0:1, :]).then_inc(dvS.h, 1)
                dvS.n += 1
            for i in range(NT):
                abase = ac_A[i] - 2
                vector.wait_ge(ldx.h, ldx0 + 16 * (i + 1))
                vector.tensor_reduce(stat[:, 0:1], xt[i % 2][:], axis=AX.X, op=ALU.add).then_inc(dvS.h, 1)
                dvS.n += 1
                vector.wait_ge(dvS.h, dvS.n)
                vector.tensor_scalar_mul(stat[:, 2:3], stat[:, 0:1], -1.0 / D).then_inc(dvS.h, 1)
                dvS.n += 1
                vector.wait_ge(dvS.h, dvS.n)
                vector.tensor_tensor(stat[:, 3:4], stat[:, 2:3], stat[:, 2:3], ALU.mult).then_inc(dvS.h, 1)
                dvS.n += 1
                vector.wait_ge(acS.h, abase + 1)
                vector.wait_ge(dvS.h, dvS.n)
                vector.tensor_scalar(stat[:, 4:5], stat[:, 1:2], 1.0 / D, EPS_LN,
                                     ALU.mult, ALU.add).then_inc(dvS.h, 1)
                dvS.n += 1
                vector.wait_ge(dvS.h, dvS.n)
                vector.tensor_tensor(stat[:, 5:6], stat[:, 4:5], stat[:, 3:4], ALU.subtract).then_inc(dvS.h, 1)
                dvS.n += 1
                vector.wait_ge(acS.h, abase + 2)
                vector.reciprocal(stat[:, 7:8], stat[:, 6:7]).then_inc(dvS.h, 1)
                dvS.n += 1
                vector.wait_ge(dvS.h, dvS.n)
                vector.tensor_scalar(sqs[:], xt[i % 2][:], stat[:, 2:3], stat[:, 7:8],
                                     ALU.add, ALU.mult).then_inc(dvS.h, 1)
                dvS.n += 1
                vector.wait_ge(dvS.h, dvS.n)
                if i >= 2:
                    vector.wait_ge(stA.h, st_A[i - 2])
                vector.tensor_tensor(ht[i % 2][:], sqs[:], gb_sb[:], ALU.mult).then_inc(dvS.h, 1)
                dvS.n += 1
                vector.engine_nop().then_inc(dvS.h, 1)
                dvS.n += 1
                assert dvS.n == dv_A[i]

        @blk.scalar
        def _(scalar):
            for i in range(NT):
                base = dv_A[i] - 9
                scalar.wait_ge(ldx.h, ldx0 + 16 * (i + 1))
                if i >= 1:
                    scalar.wait_ge(dvS.h, dv_A[i - 1])
                scalar.activation(sqs[:], xt[i % 2][:], AF.Square,
                                  accum_out=stat[:, 1:2]).then_inc(acS.h, 1)
                scalar.wait_ge(dvS.h, base + 5)
                scalar.activation(stat[:, 6:7], stat[:, 5:6], AF.Sqrt).then_inc(acS.h, 1)
        acS.n = ac_A[NT - 1]

        @blk.gpsimd
        def _(gpsimd):
            gpsimd.wait_ge(ldhT.h, ldhT0)
            for i in range(NT):
                gpsimd.wait_ge(dvS.h, dv_A[i])
                gpsimd.dma_start(out=hln_d[i * 128:(i + 1) * 128, :], in_=ht[i % 2][:]).then_inc(stA.h, 16)
        stA.n = ST_A_END

    esB = contextlib.ExitStack()
    sbB = lambda n, s, d: esB.enter_context(nc.sbuf_tensor(b.pfx + f"s{l}_" + n, s, d))
    qrow = [sbB(f"qrow{jc}", [128, T], F16) for jc in range(8)]
    ksm_sb = [sbB(f"ksm_sb{p}", [128, D], BF16) for p in range(2)]
    vsb = [sbB(f"vsb{p}", [128, 512], BF16) for p in range(2)]

    with nc.Block() as blk:
        @blk.sync
        def _(sync):
            for tb in range(8):
                sync.wait_ge(stA.h, stA0 + 16 * min(4 * tb + 4, NT))
                if tb >= 2:
                    sync.wait_ge(peS.h, pe_v[(tb - 2, 7)])
                else:
                    sync.wait_ge(peS.h, max(0, peB0 - 212 + tb * 20))
                for k in range(8):
                    sync.dma_start(out=hlnT[tb % 2][k][:],
                                   in_=hln_d[tb * 512:(tb + 1) * 512, k * 128:(k + 1) * 128],
                                   transpose=True).then_inc(ldhT.h, 16)
        ldhT.n = LD_HT_END

        @blk.tensor
        def _(tensor):
            for tb in range(8):
                tensor.wait_ge(ldhT.h, ld_hT[tb])
                for jc in range(8):
                    if tb == 0:
                        if jc >= 4:
                            tensor.wait_ge(dvS.h, dv_pre[f'bw{5}'])
                    else:
                        tensor.wait_ge(dvS.h, dv_ve[(tb - 1, jc)])
                        if jc >= 4:
                            tensor.wait_ge(acS.h, ac_k[(tb - 1, 3)])
                    for k in range(8):
                        tensor.matmul(pq[jc][:],
                                      wqkv_sb[k][jc // 4][:, (jc % 4) * 128:(jc % 4) * 128 + 128],
                                      hlnT[tb % 2][k][:], start=(k == 0), stop=False)
                    mm = tensor.matmul(pq[jc][:], bW_b[0:1, jc * 128:(jc + 1) * 128],
                                       onesb5_sb[:], start=False, stop=True)
                    mm.then_inc(peS.h, 1)
                for tt in range(4):
                    b0 = 4 + 2 * (tt % 2)
                    tensor.wait_ge(dvS.h, dv_qe[(tb, b0)])
                    tensor.wait_ge(dvS.h, dv_qe[(tb, b0 + 1)])
                    if tt >= 2:
                        tensor.wait_ge(acS.h, ac_k[(tb, tt - 2)])
                    mm = None
                    for jc2 in range(2):
                        for k in range(8):
                            tensor.matmul(pq[b0 + jc2][:],
                                          hlnT[tb % 2][k][:, tt * 128:(tt + 1) * 128],
                                          wqkv_sb[k][2 + jc2][:], start=(k == 0), stop=False)
                        mm = tensor.matmul(pq[b0 + jc2][:], onesb_sb[:],
                                           bW_b[0:1, D + jc2 * 512:D + (jc2 + 1) * 512],
                                           start=False, stop=True)
                    mm.then_inc(peS.h, 1)
                for jc in range(8):
                    if jc < 4:
                        tensor.wait_ge(dvS.h, dv_qe[(tb, jc)])
                    else:
                        tensor.wait_ge(acS.h, ac_k[(tb, 3)])
                        tensor.wait_ge(dvS.h, dv_k[(tb, 3)])
                    vjc = jc * 128
                    for k in range(8):
                        tensor.matmul(pq[jc][:],
                                      wqkv_sb[k][4 + jc // 4][:, (jc % 4) * 128:(jc % 4) * 128 + 128],
                                      hlnT[tb % 2][k][:], start=(k == 0), stop=False)
                    mm = tensor.matmul(pq[jc][:], bW_b[0:1, 2 * D + vjc:2 * D + vjc + 128],
                                       onesb5_sb[:], start=False, stop=True)
                    mm.then_inc(peS.h, 1)

        @blk.vector
        def _(vector):
            for tb in range(8):
                for jc in range(8):
                    vector.wait_ge(peS.h, pe_q[(tb, jc)])
                    vector.tensor_copy(qrow[jc][:, tb * 512:(tb + 1) * 512],
                                       pq[jc][:]).then_inc(dvS.h, 1)
                    dvS.n += 1
                for tt in range(4):
                    b0 = 4 + 2 * (tt % 2)
                    g = tb * 4 + tt
                    vector.wait_ge(peS.h, pe_k[(tb, tt)])
                    vector.reduce_max(kst[:, 0:1], pq[b0][:], axis=AX.X).then_inc(dvS.h, 1)
                    dvS.n += 1
                    vector.reduce_max(kst[:, 1:2], pq[b0 + 1][:], axis=AX.X).then_inc(dvS.h, 1)
                    dvS.n += 1
                    vector.wait_ge(dvS.h, dvS.n)
                    vector.tensor_tensor(kst[:, 2:3], kst[:, 0:1], kst[:, 1:2], ALU.max).then_inc(dvS.h, 1)
                    dvS.n += 1
                    vector.wait_ge(dvS.h, dvS.n)
                    vector.tensor_scalar_mul(kst[:, 3:4], kst[:, 2:3], -1.0).then_inc(dvS.h, 1)
                    dvS.n += 1
                    vector.wait_ge(acS.h, ac_k[(tb, tt)])
                    vector.tensor_tensor(kst[:, 6:7], kst[:, 4:5], kst[:, 5:6], ALU.add).then_inc(dvS.h, 1)
                    dvS.n += 1
                    vector.wait_ge(dvS.h, dvS.n)
                    vector.reciprocal(kst[:, 7:8], kst[:, 6:7]).then_inc(dvS.h, 1)
                    dvS.n += 1
                    vector.wait_ge(dvS.h, dvS.n)
                    if g >= 2:
                        vector.wait_ge(stB.h, st_k[((g - 2) // 4, (g - 2) % 4)])
                    vector.tensor_single_scalar(ksm_sb[g % 2][:], ksm_sb[g % 2][:], kst[:, 7:8],
                                                ALU.mult).then_inc(dvS.h, 1)
                    dvS.n += 1
                    assert dvS.n == dv_k[(tb, tt)]
                for jc in range(8):
                    g = tb * 8 + jc
                    vector.wait_ge(peS.h, pe_v[(tb, jc)])
                    if g >= 2:
                        pg = g - 2
                        vector.wait_ge(stB.h, st_v[(pg // 8, pg % 8)])
                    vector.tensor_copy(vsb[g % 2][:], pq[jc][:]).then_inc(dvS.h, 1)
                    dvS.n += 1
                    assert dvS.n == dv_ve[(tb, jc)]
            for jc in range(8):
                vector.wait_ge(dvS.h, dv_qe[(7, jc)])
                vector.reduce_max(qst[:, 0:1], qrow[jc][:], axis=AX.X).then_inc(dvS.h, 1)
                dvS.n += 1
                vector.wait_ge(dvS.h, dvS.n)
                vector.tensor_scalar_mul(qst[:, 1:2], qst[:, 0:1], -1.0).then_inc(dvS.h, 1)
                dvS.n += 1
                vector.wait_ge(acS.h, ac_q[jc])
                vector.reciprocal(qst[:, 2:3], qst[:, 3:4]).then_inc(dvS.h, 1)
                dvS.n += 1
                vector.wait_ge(dvS.h, dvS.n)
                vector.tensor_single_scalar(qrow[jc][:], qrow[jc][:], qst[:, 2:3],
                                            ALU.mult).then_inc(dvS.h, 1)
                dvS.n += 1
                assert dvS.n == dv_qs[jc]

        @blk.scalar
        def _(scalar):
            for tb in range(8):
                for tt in range(4):
                    b0 = 4 + 2 * (tt % 2)
                    g = tb * 4 + tt
                    scalar.wait_ge(dvS.h, dv_k[(tb, tt)] - 4)
                    if g >= 2:
                        pg = g - 2
                        scalar.wait_ge(stB.h, st_k[(pg // 4, pg % 4)])
                    scalar.activation(ksm_sb[g % 2][:, 0:512], pq[b0][:], AF.Exp, bias=kst[:, 3:4],
                                      accum_out=kst[:, 4:5]).then_inc(acS.h, 1)
                    scalar.activation(ksm_sb[g % 2][:, 512:1024], pq[b0 + 1][:], AF.Exp, bias=kst[:, 3:4],
                                      accum_out=kst[:, 5:6]).then_inc(acS.h, 1)
            for jc in range(8):
                scalar.wait_ge(dvS.h, dv_qs[jc] - 2)
                scalar.activation(qrow[jc][:], qrow[jc][:], AF.Exp, bias=qst[:, 1:2],
                                  accum_out=qst[:, 3:4]).then_inc(acS.h, 1)
        acS.n = AC_END

        @blk.gpsimd
        def _(gpsimd):
            gpsimd.wait_ge(ldC.h, ldC0)
            gpsimd.wait_ge(ldD.h, ldD0)
            for tb in range(8):
                for tt in range(4):
                    g = tb * 4 + tt
                    gpsimd.wait_ge(dvS.h, dv_k[(tb, tt)])
                    r0 = (tb * 4 + tt) * 128
                    gpsimd.dma_start(out=ks_d[r0:r0 + 128, :],
                                     in_=ksm_sb[g % 2][:]).then_inc(stB.h, 16)
                for jc in range(8):
                    g = tb * 8 + jc
                    gpsimd.wait_ge(dvS.h, dv_ve[(tb, jc)])
                    gpsimd.dma_start(out=vT_d[jc * 128:(jc + 1) * 128, tb * 512:(tb + 1) * 512],
                                     in_=vsb[g % 2][:]).then_inc(stB.h, 16)
            for jc in range(8):
                gpsimd.wait_ge(dvS.h, dv_qs[jc])
                gpsimd.dma_start(out=qs_d[jc * 128:(jc + 1) * 128, :],
                                 in_=qrow[jc][:]).then_inc(stB.h, 16)
        stB.n = ST_B_END

    esB.close()

    esC = contextlib.ExitStack()
    sbC = lambda n, s, d: esC.enter_context(nc.sbuf_tensor(b.pfx + f"s{l}_" + n, s, d))
    B_sb = [sbC(f"B_sb{cp}", [128, D], BF16) for cp in range(8)]
    ksC = [sbC(f"ksC{p}", [128, D], BF16) for p in range(2)]
    qsC = [sbC(f"qsC{p}", [128, 512], BF16) for p in range(2)]
    ksC2 = [sbC(f"ksC2{p}", [128, D], BF16) for p in range(2)]
    qsC2 = [sbC(f"qsC2{p}", [128, 512], BF16) for p in range(2)]

    with nc.Block() as blk:
        @blk.sync
        def _(sync):
            sync.wait_ge(stB.h, ST_B_END)
            for ch in range(2):
                for i in range(0, 32, 2):
                    g = ch * 32 + i
                    if g >= 4:
                        pch, pi = divmod(g - 4, 32)
                        sync.wait_ge(peS.h, pe_C[(pch, pi)])
                    sync.dma_start(out=ksC[(g // 2) % 2][:],
                                   in_=ks_d[i * 128:(i + 1) * 128, :]).then_inc(ldC.h, 16)
                    sync.dma_start(out=qsC[(g // 2) % 2][:],
                                   in_=qs_d[ch * 512:(ch + 1) * 512, i * 128:(i + 1) * 128],
                                   transpose=True).then_inc(ldC.h, 16)

        @blk.scalar
        def _(scalar):
            scalar.wait_ge(stB.h, ST_B_END)
            for ch in range(2):
                for i in range(1, 32, 2):
                    g = ch * 32 + i
                    if g >= 4:
                        pch, pi = divmod(g - 4, 32)
                        scalar.wait_ge(peS.h, pe_C[(pch, pi)])
                    scalar.dma_start(out=ksC2[(g // 2) % 2][:],
                                     in_=ks_d[i * 128:(i + 1) * 128, :]).then_inc(ldD.h, 16)
                    scalar.dma_start(out=qsC2[(g // 2) % 2][:],
                                     in_=qs_d[ch * 512:(ch + 1) * 512, i * 128:(i + 1) * 128],
                                     transpose=True).then_inc(ldD.h, 16)
        ldC.n = LD_C_END

        @blk.tensor
        def _(tensor):
            for ch in range(2):
                if ch == 0:
                    tensor.wait_ge(dvS.h, dv_ve[(7, 7)])
                    tensor.wait_ge(dvS.h, dv_qs[7])
                else:
                    tensor.wait_ge(dvS.h, dv_C[(0, 7)])
                for i in range(32):
                    g = ch * 32 + i
                    if i % 2 == 0:
                        tensor.wait_ge(ldC.h, ld_C[(ch, i // 2)])
                        kt, qt_ = ksC[(g // 2) % 2], qsC[(g // 2) % 2]
                    else:
                        tensor.wait_ge(ldD.h, ld_Codd[(ch, i // 2)])
                        kt, qt_ = ksC2[(g // 2) % 2], qsC2[(g // 2) % 2]
                    for cp in range(8):
                        mm = tensor.matmul(pq[cp][:], kt[:, cp * 128:(cp + 1) * 128],
                                           qt_[:], start=(i == 0), stop=(i == 31))
                    mm.then_inc(peS.h, 1)

        @blk.vector
        def _(vector):
            for ch in range(2):
                for cp in range(8):
                    vector.wait_ge(peS.h, pe_C[(ch, 31)])
                    vector.tensor_copy(B_sb[cp][:, ch * 512:(ch + 1) * 512], pq[cp][:]).then_inc(dvS.h, 1)
                    dvS.n += 1
                    assert dvS.n == dv_C[(ch, cp)]

    esD = contextlib.ExitStack()
    sbD = lambda n, s, d: esD.enter_context(nc.sbuf_tensor(b.pfx + f"s{l}_" + n, s, d))
    oT_sb = [sbD(f"oT_sb{cc}", [128, 512], BF16) for cc in range(8)]
    wout_sb = [[sbD(f"wout_sb{k}_{f}", [128, 512], BF16) for f in range(2)] for k in range(8)]
    vTD = [[sbD(f"vTD{p}_{cp}", [128, 512], BF16) for cp in range(8)] for p in range(2)]
    xrD = [sbD(f"xrD{t}", [128, D], F32) for t in range(4)]

    with nc.Block() as blk:
        @blk.sync
        def _(sync):
            for k in range(8):
                for f in range(2):
                    sync.dma_start(out=wout_sb[k][f][:],
                                   in_=wout_b[l, k * 128:(k + 1) * 128,
                                              f * 512:(f + 1) * 512]).then_inc(ldD.h, 16)
            for tb in range(8):
                if tb >= 2:
                    sync.wait_ge(peS.h, pe_E[(tb - 2, 7)])
                for cp in range(8):
                    sync.dma_start(out=vTD[tb % 2][cp][:],
                                   in_=vT_d[cp * 128:(cp + 1) * 128,
                                            tb * 512:(tb + 1) * 512]).then_inc(ldD.h, 16)
                for tsub in range(4):
                    if tb >= 1:
                        sync.wait_ge(dvS.h, dv_E[(tb - 1, tsub * 2 + 1)])
                    r0 = (tb * 4 + tsub) * 128
                    sync.dma_start(out=xrD[tsub][:],
                                   in_=x_src[r0:r0 + 128, :]).then_inc(ldD.h, 16)
        ldD.n = LD_D_END

        @blk.tensor
        def _(tensor):
            tensor.wait_ge(dvS.h, dv_C[(1, 7)])
            tensor.wait_ge(ldD.h, LD_D_W)
            for tb in range(8):
                tensor.wait_ge(ldD.h, LD_D_W + 16 * 12 * tb + 16 * 8)
                for cc in range(8):
                    if tb > 0:
                        tensor.wait_ge(dvS.h, dv_E[(tb - 1, cc)])
                    mm = None
                    for cp in range(8):
                        mm = tensor.matmul(pq[cc][:], B_sb[cp][:, cc * 128:(cc + 1) * 128],
                                           vTD[tb % 2][cp][:], start=(cp == 0), stop=(cp == 7))
                    mm.then_inc(peS.h, 1)
                tensor.wait_ge(ldD.h, LD_D_W + 16 * 12 * (tb + 1))
                for i8 in range(8):
                    tsub, c2 = divmod(i8, 2)
                    tensor.wait_ge(dvS.h, dv_D[(tb, i8)])
                    mm = None
                    for cc in range(8):
                        mm = tensor.matmul(pq[i8][:], oT_sb[cc][:, tsub * 128:(tsub + 1) * 128],
                                           wout_sb[cc][c2][:], start=(cc == 0), stop=False)
                    mm = tensor.matmul(pq[i8][:], onesb_sb[:], brow_b[0:1, c2 * 512:(c2 + 1) * 512],
                                       start=False, stop=True)
                    mm.then_inc(peS.h, 1)

        @blk.vector
        def _(vector):
            for tb in range(8):
                for cc in range(8):
                    vector.wait_ge(peS.h, pe_D[(tb, cc)])
                    vector.tensor_copy(oT_sb[cc][:], pq[cc][:]).then_inc(dvS.h, 1)
                    dvS.n += 1
                    assert dvS.n == dv_D[(tb, cc)]
                for i8 in range(8):
                    tsub, c2 = divmod(i8, 2)
                    vector.wait_ge(peS.h, pe_E[(tb, i8)])
                    g = tb * 4 + tsub
                    if g >= 2 and c2 == 0:
                        vector.wait_ge(stE.h, st_E[((g - 2) // 4, (g - 2) % 4)])
                    vector.tensor_add(xoD[g % 2][:, c2 * 512:(c2 + 1) * 512], pq[i8][:],
                                      xrD[tsub][:, c2 * 512:(c2 + 1) * 512]).then_inc(dvS.h, 1)
                    dvS.n += 1
                    assert dvS.n == dv_E[(tb, i8)]
            assert dvS.n == DV_END

        @blk.gpsimd
        def _(gpsimd):
            for tb in range(8):
                for tsub in range(4):
                    g = tb * 4 + tsub
                    gpsimd.wait_ge(dvS.h, dv_E[(tb, tsub * 2 + 1)])
                    r0 = (tb * 4 + tsub) * 128
                    gpsimd.dma_start(out=x_dst[r0:r0 + 128, :],
                                     in_=xoD[g % 2][:]).then_inc(stE.h, 16)
        stE.n = ST_E_END

    esD.close()
    esC.close()
    esA.close()
    peS.n = PE_END
    assert dvS.n == DV_END
    acS.n = AC_END



def build_full(n_emits=7, pool=False):
    b = B("full")
    nc = b.nc
    x_in = b.din("x_in", [T, D])
    wg = b.din("wg", [I0, D])
    wu = b.din("wu", [I0, D])
    wd = b.din("wd", [D, I0])
    wqT = b.din("wqT", [D, D])
    wkT = b.din("wkT", [D, D])
    wvT = b.din("wvT", [D, D])
    woT = b.din("woT", [D, D])
    bo = b.din("bo", [1, D])
    wgT = b.din("wgT", [D, E])
    w1s = b.din("w1s", [E, D, HM])
    w2s = b.din("w2s", [E, D, HM])
    w3s = b.din("w3s", [E, HM, D])
    w_qkv = b.din("w_qkv", [L1DEPTH, D, 3 * D])
    w_out = b.din("w_out", [L1DEPTH, D, D])
    b_out = b.din("b_out", [L1DEPTH, D])
    gamma = b.din("gamma", [L1DEPTH, D])
    beta_pc = b.din("beta_pc", [L1DEPTH, 128, 8])
    ones_f = b.din("ones_f", [1, 128])
    ones_b = b.din("ones_b", [1, 128], BF16)
    ones_b512 = b.din("ones_b512", [1, 512], BF16)
    ident = b.din("ident", [128, 128])
    x_out = b.dout("x_out", [T, D])

    xA = nc.dram_tensor("xA", [T, D], F32)
    xB = nc.dram_tensor("xB", [T, D], F32)
    wgq = nc.dram_tensor("wgq", [I0, D], BF16)
    wgql = nc.dram_tensor("wgql", [I0, D], BF16)
    wuq = nc.dram_tensor("wuq", [I0, D], BF16)
    wuql = nc.dram_tensor("wuql", [I0, D], BF16)
    wdq = nc.dram_tensor("wdq", [D, I0], BF16)
    w1c = nc.dram_tensor("w1c", [E, D, HM], BF16)
    w2c = nc.dram_tensor("w2c", [E, D, HM], BF16)
    w3c = nc.dram_tensor("w3c", [E, HM, D], BF16)
    wqkvc = nc.dram_tensor("wqkvc", [L1DEPTH, D, 3 * D], BF16)
    woutc = nc.dram_tensor("woutc", [L1DEPTH, D, D], BF16)
    bg = BGCast(b, "bgcast", moe_cast_jobs(w1s, w2s, w3s, w1c, w2c, w3c))
    bg2 = BGCast(b, "bgcast2", l1_cast_jobs(w_qkv, w_out, wqkvc, woutc))
    wb_shared = (w1c, w2c, w3c, bg.sem)
    wb_l1 = (wqkvc, woutc, bg2.sem)

    if pool:
        b.pool = {}
        b.bar = b.es.enter_context(nc.semaphore("pool_bar"))

    def scoped(pfx, fn):
        es = contextlib.ExitStack()
        b.pfx = pfx
        b.cur = es
        ex = fn()
        es.close()
        b.cur = b.es
        return ex

    steps = [
        ("pp_", lambda ex: emit_prep(b, wg, wu, wd, wgq, wgql, wuq, wuql, wdq, None, bg=bg)),
        ("f1_", lambda ex: emit_l0ffn(b, x_in, xA, wgq, wgql, wuq, wuql, wdq, ex, bg=bg)),
        ("at_", lambda ex: emit_l0attn(b, xA, xB, wqT, wkT, wvT, woT, bo,
                                       ones_f, ones_b, ident, ex)),
        ("f2_", lambda ex: emit_l0ffn(b, xB, xA, wgq, wgql, wuq, wuql, wdq, ex, bg=bg2)),
        ("m1_", lambda ex: emit_moe(b, xA, xB, wgT, w1s, w2s, w3s, ones_f, ident, ex,
                                    wb_shared=wb_shared)),
        ("l1_", lambda ex: emit_l1attn(b, xB, xA, w_qkv, w_out, b_out, gamma, beta_pc,
                                       ones_f, ones_b, ones_b512, ex, wb=wb_l1)),
        ("m2_", lambda ex: emit_moe(b, xA, x_out, wgT, w1s, w2s, w3s, ones_f, ident, ex,
                                    wb_shared=wb_shared)),
    ]
    ex = None
    for si, (pfx, fn) in enumerate(steps[:n_emits]):
        ex = scoped(pfx, lambda f=fn, e=ex: f(e))
    if n_emits < 7:
        srcbuf = [None, None, xA, xB, xA, xB, xA][n_emits]
        fin = b.sem("fin")
        fin2 = b.sem("fin2")
        es = contextlib.ExitStack()
        b.cur = es
        xtd = [b.cur.enter_context(nc.sbuf_tensor(f"dbg_x{p}", [128, D], F32)) for p in range(2)]
        with nc.Block() as blk:
            @blk.sync
            def _(sync):
                sync.wait_ge(ex[0], ex[1])
                for i in range(NT):
                    if i >= 2:
                        sync.wait_ge(fin2.h, 16 * (i - 1))
                    sync.dma_start(out=xtd[i % 2][:],
                                   in_=(srcbuf if srcbuf is not None else x_in)[i * 128:(i + 1) * 128, :]).then_inc(fin.h, 16)

            @blk.gpsimd
            def _(gpsimd):
                for i in range(NT):
                    gpsimd.wait_ge(fin.h, 16 * (i + 1))
                    gpsimd.dma_start(out=x_out[i * 128:(i + 1) * 128, :],
                                     in_=xtd[i % 2][:]).then_inc(fin2.h, 16)
                gpsimd.wait_ge(fin2.h, 16 * NT)
        es.close()
        b.cur = b.es
        b.es.close()
        return nc

    with nc.Block() as blk:
        @blk.gpsimd
        def _(gpsimd):
            gpsimd.wait_ge(ex[0], ex[1])

    b.es.close()
    return nc


_FULL = []


def kernel(L0_attn_x, L0_attn_Wq, L0_attn_Wk, L0_attn_Wv, L0_attn_Wo, L0_attn_bo,
           L0_ffn_w_gate, L0_ffn_w_up, L0_ffn_w_down,
           L1_attn_w_qkv, L1_attn_w_out, L1_attn_b_out, L1_attn_gamma, L1_attn_beta,
           L1_ffn_Wg, L1_ffn_W1, L1_ffn_W2, L1_ffn_W3):
    NC = 8
    cores = list(range(NC))
    asc = np.ascontiguousarray
    f32 = np.float32
    if not _FULL:
        _FULL.append(build_full())
    nc = _FULL[0]
    shared = dict(
        wg=asc(f32(L0_ffn_w_gate)), wu=asc(f32(L0_ffn_w_up)), wd=asc(f32(L0_ffn_w_down)),
        wqT=asc(f32(L0_attn_Wq).T), wkT=asc(f32(L0_attn_Wk).T),
        wvT=asc(f32(L0_attn_Wv).T), woT=asc(f32(L0_attn_Wo).T),
        bo=asc(f32(L0_attn_bo).reshape(1, -1)),
        wgT=asc(f32(L1_ffn_Wg).T),
        w1s=asc(f32(L1_ffn_W1).transpose(0, 2, 1)),
        w2s=asc(f32(L1_ffn_W2).transpose(0, 2, 1)),
        w3s=asc(f32(L1_ffn_W3).transpose(0, 2, 1)),
        w_qkv=asc(f32(L1_attn_w_qkv)), w_out=asc(f32(L1_attn_w_out)),
        b_out=asc(f32(L1_attn_b_out)), gamma=asc(f32(L1_attn_gamma)),
        beta_pc=asc(f32(L1_attn_beta).reshape(6, 8, 128).transpose(0, 2, 1)),
        ones_f=np.ones((1, 128), f32),
        ones_b=np.ones((1, 128), ml_dtypes.bfloat16),
        ones_b512=np.ones((1, 512), ml_dtypes.bfloat16),
        ident=np.eye(128, dtype=f32),
    )
    in_maps = [dict(x_in=asc(f32(L0_attn_x[i])), **shared) for i in range(NC)]
    res = run_bass_kernel_spmd(nc, in_maps, core_ids=cores).results
    return np.stack([np.asarray(r["x_out"]) for r in res], axis=0)
